# version 18
# baseline (speedup 1.0000x reference)
import numpy as np
import ml_dtypes
from contextlib import ExitStack

import concourse.bass as bass
import concourse.mybir as mybir
from concourse.bass_utils import run_bass_kernel_spmd

F32 = mybir.dt.float32
BF16 = mybir.dt.bfloat16
F8 = mybir.dt.float8e4
AF = mybir.ActivationFunctionType
ALU = mybir.AluOpType
AX = mybir.AxisListType

T, D, H, DH, E, NI = 2048, 1024, 16, 64, 16, 1024
KC = 8
NTF = 4
TF = 512
NTT = 16
NCORES = 8

_SEM_OF_ENGINE = {"tensor": "pe", "vector": "dv", "scalar": "ac"}


class _Op:
    __slots__ = ("engine", "fn", "deps", "sem", "val", "inc")

    def __init__(self, engine, fn, deps, sem, val, inc):
        self.engine, self.fn, self.deps = engine, fn, deps
        self.sem, self.val, self.inc = sem, val, inc


class Sched:

    def __init__(self):
        self.ops = []
        self.counts = {}
        self.bar = []
        self.last = {}

    def op(self, engine, fn, deps=(), dma=False, sem=None):
        if dma:
            sem = sem or ("ldg" if engine == "gpsimd" else "lds")
            inc = 16
        else:
            sem = _SEM_OF_ENGINE[engine]
            inc = 1
        c = self.counts.get(sem, 0) + inc
        self.counts[sem] = c
        dd = [d for d in deps if d is not None] + self.bar
        o = _Op(engine, fn, dd, sem, c, inc)
        self.ops.append(o)
        self.last[engine] = o
        return o

    def wait_only(self, engine, deps):
        o = _Op(engine, None, [d for d in deps if d is not None], None, None, 0)
        self.ops.append(o)
        return o

    def barrier(self):
        self.bar = list(self.last.values())

    def emit(self, engine_name, eng, sems):
        hw = {}
        for o in self.ops:
            if o.engine != engine_name:
                continue
            for d in o.deps:
                if hw.get(d.sem, 0) < d.val:
                    eng.wait_ge(sems[d.sem], d.val)
                    hw[d.sem] = d.val
            if o.fn is not None:
                last = o.fn(eng)
                last.then_inc(sems[o.sem], o.inc)


def _host_consts():
    bf = ml_dtypes.bfloat16
    inv = 1.0 / 10000.0 ** (np.arange(0, DH, 2, dtype=np.float64) / DH)
    fr = np.arange(T, dtype=np.float64)[:, None] * inv[None, :]
    emb = np.concatenate([fr, fr], -1)
    cosT = np.tile(np.cos(emb).T.astype(np.float32), (2, 1))
    sinT = np.tile(np.sin(emb).T.astype(np.float32), (2, 1))
    mrot = np.zeros((128, 128), np.float32)
    for m in range(128):
        if m % 64 < 32:
            mrot[(m + 32), m] = -1.0
        else:
            mrot[(m - 32), m] = 1.0
    tri = np.triu(np.ones((128, 128), np.float32))
    ident = np.eye(128, dtype=np.float32)
    econ = np.zeros((16, 16 * 128), np.float32)
    for e in range(16):
        econ[e, e * 128:(e + 1) * 128] = 1.0
    return {
        "cosT": cosT, "sinT": sinT,
        "mrot": mrot.astype(bf), "tri": tri.astype(bf), "ident": ident.astype(bf),
        "econ": econ.astype(bf),
        "onesrow": np.ones((1, 128), np.float32),
        "negrow": -np.ones((1, 128), np.float32),
    }


def build_nc(upto=7, dbg=()):
    nc = bass.Bass("TRN2", target_bir_lowering=False)

    def din(name, shape, dt=F32):
        return nc.dram_tensor(name, list(shape), dt, kind="ExternalInput").ap()

    xT = din("xT", (D, T))
    wqT = din("wqT", (D, D)); wkT = din("wkT", (D, D)); wvT = din("wvT", (D, D))
    woT = din("woT", (D, D))
    sgT = din("sgT", (D, NI)); suT = din("suT", (D, NI)); sdT = din("sdT", (NI, D))
    egT = din("egT", (E, D, NI)); euT = din("euT", (E, D, NI)); edT = din("edT", (E, NI, D))
    rwT = din("rwT", (D, E))
    ffnK = din("ffnK", (D, D), BF16)
    n1pp_h = din("n1pp_h", (128, KC)); n2pp_h = din("n2pp_h", (128, KC))
    biaspp_h = din("biaspp_h", (128, KC))
    ebias_h = din("ebias_h", (1, E))
    scl_h = din("scl_h", (1, 2))
    cosT_h = din("cosT_h", (128, T)); sinT_h = din("sinT_h", (128, T))
    mrot_h = din("mrot_h", (128, 128), BF16); tri_h = din("tri_h", (128, 128), BF16)
    ident_h = din("ident_h", (128, 128), BF16); econ_h = din("econ_h", (16, 16 * 128), BF16)
    onesrow_h = din("onesrow_h", (1, 128)); negrow_h = din("negrow_h", (1, 128))
    miscrow_h = din("miscrow_h", (1, 4))

    outT = nc.dram_tensor("outT", [D, T], F32, kind="ExternalOutput").ap()
    dbg_out = {}
    for name, shape, dt in (
        ("d_xn", (D, T), BF16), ("d_h", (D, T), BF16), ("d_q", (256, T), BF16),
        ("d_k", (256, T), BF16), ("d_v", (T, 260), BF16), ("d_ctx", (D, T), BF16),
        ("d_racc", (D, T), F32), ("d_h2", (D, T), BF16), ("d_cw", (E, T), BF16),
        ("d_g", (D, T), BF16), ("d_s", (D, T), F32),
    ):
        if name in dbg:
            dbg_out[name] = nc.dram_tensor(name, list(shape), dt, kind="ExternalOutput").ap()

    s = Sched()
    store_ops = []

    def dbg_store(name, sb_ap, deps, rearr=None):
        if name in dbg_out:
            tgt = dbg_out[name]
            if rearr is not None:
                tgt = tgt.rearrange(rearr, p=128)
            o = s.op("sync", lambda e, a=sb_ap, t=tgt: e.dma_start(t, a),
                     deps=list(deps), dma=True)
            store_ops.append(o)

    esL = ExitStack()
    esR0 = ExitStack()

    def sbL(name, shape, dt):
        return esL.enter_context(nc.sbuf_tensor(name, list(shape), dt, side="left"))

    def sbR(es, name, shape, dt):
        return es.enter_context(nc.sbuf_tensor(name, list(shape), dt, side="right"))

    mrot = sbL("mrot", (128, 128), BF16)
    tri = sbL("tri", (128, 128), BF16)
    ident = sbL("ident", (128, 128), BF16)
    onescol = sbL("onescol", (128, 1), BF16)
    onesrow = sbL("onesrow", (1, 128), F32)
    negrow = sbL("negrow", (1, 128), F32)
    n1pp = sbL("n1pp", (128, KC), F32)
    n2pp = sbL("n2pp", (128, KC), F32)
    biaspp = sbL("biaspp", (128, KC), F32)
    ebias = sbL("ebias", (1, E), F32)
    scl = sbL("scl", (1, 2), F32)
    srow = sbL("srow", (1, 8), F32)
    miscrow = sbL("miscrow", (1, 4), F32)
    onecolf = sbL("onecolf", (128, 1), F32)
    spp = sbL("spp", (128, 4), F32)

    esR1 = ExitStack()
    esR2 = ExitStack()
    esRc = ExitStack()
    xnq = sbR(esR0, "xnq", (128, KC, T), F8)
    xn = sbR(esR1, "xn", (128, KC, T), BF16)
    hsb = sbR(esR2, "hsb", (128, KC, T), BF16)
    ctxT = sbR(esR2, "ctxT", (128, KC, T), BF16)
    cos_sb = sbR(esRc, "cos_sb", (128, T), F32)
    sin_sb = sbR(esRc, "sin_sb", (128, T), F32)

    ld_consts = []
    for ap_h, sb in ((cosT_h, cos_sb), (sinT_h, sin_sb), (mrot_h, mrot), (tri_h, tri),
                     (ident_h, ident), (onesrow_h, onesrow),
                     (negrow_h, negrow), (n1pp_h, n1pp), (n2pp_h, n2pp),
                     (biaspp_h, biaspp), (ebias_h, ebias), (scl_h, scl),
                     (miscrow_h, miscrow)):
        ld_consts.append(s.op("sync", lambda e, a=ap_h, t=sb: e.dma_start(t[:, :], a),
                              dma=True, sem="ldc"))
    op_ones = s.op("vector", lambda e: e.memset(onescol[:, :], 1.0))
    op_onesf = s.op("vector", lambda e: e.memset(onecolf[:, :], 1.0))
    s.bar = [ld_consts[-1], op_onesf]

    esP1 = ExitStack()
    xbf = sbR(esP1, "xbf", (128, KC, T), BF16)
    rows = sbR(esP1, "rows", (1, 4, T), F32)
    bct = sbR(esP1, "bct", (128, 3, TF), F32)
    sq_sc = sbR(esP1, "sq_sc", (128, 2, TF), BF16)
    t1_sc = sbR(esP1, "t1_sc", (128, 2, TF), F32)

    op_ldx = s.op("gpsimd", lambda e: e.dma_start(
        xbf[:, :, :], xT.rearrange("(c p) t -> p c t", p=128)), dma=True, sem="ldx")
    op_ldx2 = [op_ldx, op_ldx]

    esPs1 = ExitStack()
    ps_sum = esPs1.enter_context(nc.psum_tensor("ps_sum", [1, T], F32))
    ps_sq = esPs1.enter_context(nc.psum_tensor("ps_sq", [1, T], F32))

    sq_ops = []
    last_sq_w = {}
    for tf in range(NTF):
        sl = slice(tf * TF, (tf + 1) * TF)
        for kc in range(KC):
            par = kc % 2
            op_sq = s.op("vector", lambda e, kc=kc, sl=sl, par=par: e.tensor_tensor(
                sq_sc[:, par, :], xbf[:, kc, sl], xbf[:, kc, sl], ALU.mult),
                deps=[op_ldx2[kc // 4], op_ones, last_sq_w.get(par)])
            op_m1 = s.op("tensor", lambda e, kc=kc, sl=sl: e.matmul(
                ps_sum[0:1, sl], onescol[:, :], xbf[:, kc, sl],
                start=(kc == 0), stop=(kc == KC - 1)), deps=[op_ldx2[kc // 4], op_ones])
            op_m2 = s.op("tensor", lambda e, par=par, sl=sl, kc=kc: e.matmul(
                ps_sq[0:1, sl], onescol[:, :], sq_sc[:, par, :],
                start=(kc == 0), stop=(kc == KC - 1)), deps=[op_sq])
            last_sq_w[par] = op_m2
            if kc == KC - 1:
                sq_ops.append((op_m1, op_m2))
    stat_deps = [o for pair in sq_ops for o in pair]
    r = lambda i: rows[0:1, i, :]
    op_mean = s.op("scalar", lambda e: e.activation(r(0), ps_sum[0:1, :], AF.Copy,
                   bias=0.0, scale=1.0 / D), deps=stat_deps)
    op_msq = s.op("scalar", lambda e: e.activation(r(2), ps_sq[0:1, :], AF.Copy,
                  bias=0.0, scale=1.0 / D), deps=stat_deps)
    op_m2r = s.op("vector", lambda e: e.tensor_tensor(r(1), r(0), r(0), ALU.mult),
                  deps=[op_mean])
    op_var = s.op("vector", lambda e: e.tensor_tensor(r(1), r(2), r(1), ALU.subtract),
                  deps=[op_msq, op_m2r])
    op_lnv = s.op("scalar", lambda e: e.activation(r(2), r(1), AF.Ln,
                  bias=miscrow[0:1, 0:1], scale=1.0), deps=[op_var] + ld_consts[-1:])
    op_a = s.op("scalar", lambda e: e.activation(r(3), r(2), AF.Exp, bias=0.0,
                scale=-0.5), deps=[op_lnv])
    op_b = s.op("vector", lambda e: e.tensor_tensor(r(0), r(0), r(3), ALU.mult),
                deps=[op_a])
    op_e1 = s.op("scalar", lambda e: e.activation(r(2), r(2), AF.Exp, bias=0.0,
                 scale=-1.0), deps=[op_lnv, op_a])
    op_ratio = s.op("vector", lambda e: e.tensor_tensor(r(1), r(1), r(2), ALU.mult),
                    deps=[op_e1, op_var])
    op_lnr = s.op("scalar", lambda e: e.activation(r(1), r(1), AF.Ln,
                  bias=miscrow[0:1, 1:2], scale=1.0), deps=[op_ratio])
    op_rms = s.op("scalar", lambda e: e.activation(r(1), r(1), AF.Exp, bias=0.0,
                  scale=-0.5), deps=[op_lnr])
    esPs1.close()

    es_bc = ExitStack()
    ps_bc = [es_bc.enter_context(nc.psum_tensor(f"ps_bc{i}", [128, TF], F32))
             for i in range(2)]
    op_xn_done = []
    last_t1 = {}
    last_bct = None
    for tf in range(NTF):
        sl = slice(tf * TF, (tf + 1) * TF)
        bc_ops = []
        for i, (row_i, dep) in enumerate(((3, op_a), (0, op_b), (1, op_rms))):
            par = i % 2
            op_bm = s.op("tensor", lambda e, row_i=row_i, sl=sl, par=par: e.matmul(
                ps_bc[par][:, :], onesrow[0:1, :], rows[0:1, row_i, sl],
                start=True, stop=True), deps=[dep, bc_ops[-2] if i >= 2 else None,
                                              last_bct])
            op_bc = s.op("vector", lambda e, i=i, sl=sl, par=par: e.tensor_copy(
                bct[:, i, :], ps_bc[par][:, :]), deps=[op_bm])
            bc_ops.append(op_bc)
        for kc in range(KC):
            par = kc % 2
            op_t1 = s.op("vector", lambda e, kc=kc, sl=sl, par=par: e.tensor_tensor(
                t1_sc[:, par, :], xbf[:, kc, sl], bct[:, 0, :], ALU.mult),
                deps=bc_ops + [last_t1.get(par)])
            op_xn = s.op("vector", lambda e, kc=kc, sl=sl, par=par: e.tensor_tensor(
                xn[:, kc, sl], t1_sc[:, par, :], bct[:, 1, :], ALU.subtract),
                deps=[op_t1])
            last_t1[par] = op_xn
            op_h = s.op("vector", lambda e, kc=kc, sl=sl: e.scalar_tensor_tensor(
                hsb[:, kc, sl], bct[:, 2, :], n1pp[:, kc:kc + 1], xn[:, kc, sl],
                ALU.mult, ALU.mult), deps=[op_xn])
            op_xn_done.append(op_h)
            last_bct = op_h
    dbg_store("d_xn", xn[:, :, :], op_xn_done, "(c p) t -> p c t")
    dbg_store("d_h", hsb[:, :, :], op_xn_done, "(c p) t -> p c t")
    es_bc.close()
    s.barrier()
    esP1.close()
    if upto <= 1:
        return _finish(nc, s, store_ops, outT, xn, dbg_out, esL, esR0,
                       (esRc, esR2, esR1))

    qkv_done = []
    esQ = ExitStack()
    wq_sb = sbR(esQ, "wq_sb", (128, KC, 256), BF16)
    wk_sb = sbR(esQ, "wk_sb", (128, KC, 256), BF16)
    wv_sb = sbR(esQ, "wv_sb", (128, KC, 256), BF16)
    qrot = sbR(esQ, "qrot", (128, 2, T), BF16)
    krot = sbR(esQ, "krot", (128, 2, T), BF16)
    vsb = sbR(esQ, "vsb", (128, NTT, 260), BF16)
    qtmp = sbR(esQ, "qtmp", (128, T), BF16)
    rtmp = sbR(esQ, "rtmp", (128, 4, TF), F32)
    pt_sc = sbR(esQ, "pt_sc", (128, 8, TF), BF16)
    chpp = sbR(esQ, "chpp", (128, 4), F32)
    qsq = sbR(esQ, "qsq", (128, T), BF16)
    lb_sb = sbR(esQ, "lb_sb", (64, TF), F32)
    lrow = sbR(esQ, "lrow", (1, TF), F32)
    kmx = sbR(esQ, "kmx", (1, 16), F32)
    for quarter in range(4):
        ecols = slice(quarter * 256, (quarter + 1) * 256)
        ld_q = s.op("gpsimd", lambda e, ecols=ecols, t=wq_sb: e.dma_start(
            t[:, :, :], wqT.rearrange("(c p) e -> p c e", p=128)[:, :, ecols]),
            dma=True, sem="lwq")
        ld_k = s.op("gpsimd", lambda e, ecols=ecols, t=wk_sb: e.dma_start(
            t[:, :, :], wkT.rearrange("(c p) e -> p c e", p=128)[:, :, ecols]),
            dma=True, sem="lwk")
        ld_v = s.op("gpsimd", lambda e, ecols=ecols, t=wv_sb: e.dma_start(
            t[:, :, :], wvT.rearrange("(c p) e -> p c e", p=128)[:, :, ecols]),
            dma=True, sem="lwv")

        esPq1 = ExitStack()
        ps_mm = [esPq1.enter_context(nc.psum_tensor(f"ps_mm{quarter}_{i}", [128, TF], F32))
                 for i in range(2)]
        ps_rot = [esPq1.enter_context(nc.psum_tensor(f"ps_rot{quarter}_{i}", [128, TF], F32))
                  for i in range(2)]
        last_mm = {}
        last_rot = {}
        last_qtmp = {}
        rope_done = []
        pend_r = None
        for which, wsb, ldw, rot_dst in (("q", wq_sb, ld_q, qrot), ("k", wk_sb, ld_k, krot)):
            for ml in range(2):
                for tf in range(NTF):
                    sl = slice(tf * TF, (tf + 1) * TF)
                    par = tf % 2
                    def mmfn(e, wsb=wsb, ml=ml, sl=sl, par=par):
                        for kc in range(KC):
                            i = e.matmul(ps_mm[par][:, :],
                                         wsb[:, kc, ml * 128:(ml + 1) * 128],
                                         hsb[:, kc, sl],
                                         start=(kc == 0), stop=(kc == KC - 1))
                        return i
                    op_mm = s.op("tensor", mmfn,
                                 deps=[ldw, last_mm.get(par)] + op_xn_done[-1:])
                    op_ev = s.op("vector", lambda e, sl=sl, par=par: e.tensor_copy(
                        qtmp[:, sl], ps_mm[par][:, :]),
                        deps=[op_mm, last_qtmp.get(tf)])
                    last_mm[par] = op_ev
                    if pend_r is not None:
                        pend_r()
                        pend_r = None

                    def mk_rot(sl=sl, par=par, tf=tf, ml=ml, rot_dst=rot_dst,
                               op_ev=op_ev):
                        op_pr = s.op("tensor", lambda e: e.matmul(
                            ps_rot[par][:, :], mrot[:, :], qtmp[:, sl],
                            start=True, stop=True), deps=[op_ev, last_rot.get(par)])
                        last_qtmp[tf] = op_pr
                        op_sin = s.op("vector", lambda e: e.tensor_tensor(
                            rtmp[:, par, :], ps_rot[par][:, :], sin_sb[:, sl], ALU.mult),
                            deps=[op_pr])
                        last_rot[par] = op_sin
                        op_cos = s.op("vector", lambda e: e.tensor_tensor(
                            rtmp[:, 2 + par, :], qtmp[:, sl], cos_sb[:, sl], ALU.mult),
                            deps=[op_ev])
                        op_fin = s.op("vector", lambda e: e.tensor_tensor(
                            rot_dst[:, ml, sl], rtmp[:, 2 + par, :],
                            rtmp[:, par, :], ALU.add), deps=[op_sin, op_cos])
                        rope_done.append(op_fin)
                    pend_r = mk_rot
        pend_r()
        pend_r = None
        esPq1.close()
        esPq = ExitStack()
        ps_v = [esPq.enter_context(nc.psum_tensor(f"ps_v{quarter}_{i}", [128, 256], F32))
                for i in range(2)]
        ps_n = esPq.enter_context(nc.psum_tensor(f"ps_nn{quarter}", [1, 4, TF], F32))
        v_done = []
        last_v = {}
        op_vones = s.op("vector", lambda e: e.memset(
            vsb.ap().rearrange("p c (h x) -> p c h x", x=65)[:, :, :, 64:65], 1.0))
        ch_done = [None] * 4
        last_red = None
        last_nq = {}
        prev_lnb = None
        prev_chb = None

        def norm_pair(pairi):
            nonlocal last_red, prev_lnb, prev_chb
            ml = pairi
            for src_rot, outoff in ((krot, 0), (qrot, 1)):
                sq_ops = []
                for hb in range(2):
                    rsl = slice(hb * 64, hb * 64 + 64)
                    op_sq = s.op("vector", lambda e, rsl=rsl, ml=ml, src_rot=src_rot:
                                 e.tensor_tensor(qsq[rsl, :], src_rot[rsl, ml, :],
                                                 src_rot[rsl, ml, :], ALU.mult),
                                 deps=rope_done + [last_red])
                    sq_ops.append(op_sq)
                reds = {0: [], 1: []}
                for tf in range(NTF):
                    for hb in range(2):
                        rsl = slice(hb * 64, hb * 64 + 64)
                        nb = 2 * hb + tf % 2
                        op_nm = s.op("tensor", lambda e, rsl=rsl, tf=tf, nb=nb:
                                     e.matmul(ps_n[0:1, nb, :], onescol[rsl, 0:1],
                                              qsq[rsl, tf * TF:(tf + 1) * TF],
                                              start=True, stop=True),
                                     deps=[sq_ops[hb], last_nq.get(nb)])
                        op_rd = s.op("vector", lambda e, tf=tf, hb=hb, nb=nb:
                                     e.tensor_reduce(
                            kmx[0:1, 2 + 6 * hb + tf:3 + 6 * hb + tf],
                            ps_n[0:1, nb, :], AX.X, ALU.max),
                            deps=[op_nm, prev_lnb])
                        last_nq[nb] = op_rd
                        reds[hb].append(op_rd)
                for hb in range(2):
                    col = outoff + 8 * hb
                    op_mx = s.op("vector", lambda e, col=col, hb=hb: e.tensor_reduce(
                        kmx[0:1, col:col + 1], kmx[0:1, 2 + 6 * hb:6 + 6 * hb],
                        AX.X, ALU.max), deps=reds[hb])
                    last_red = op_mx
            for hb in range(2):
                hl = 2 * pairi + hb
                c0 = 8 * hb
                op_lnb = s.op("scalar", lambda e, c0=c0: e.activation(
                    srow[0:1, 0:2], kmx[0:1, c0:c0 + 2], AF.Ln, bias=0.0, scale=1.0),
                    deps=[last_red, prev_chb])
                prev_lnb = op_lnb
                op_ch1 = s.op("vector", lambda e: e.tensor_tensor(
                    srow[0:1, 2:3], srow[0:1, 0:1], srow[0:1, 1:2], ALU.add),
                    deps=[op_lnb, prev_chb])
                op_ch = s.op("vector", lambda e: e.tensor_scalar(
                    srow[0:1, 2:3], srow[0:1, 2:3], -0.5, float(np.log(8.0)),
                    ALU.mult, ALU.add), deps=[op_ch1])
                op_chb = s.op("tensor", lambda e: e.matmul(
                    ps_v[0][:, 0:1], onesrow[0:1, :], srow[0:1, 2:3],
                    start=True, stop=True), deps=[op_ch, last_v.get(0)])
                op_chc = s.op("vector", lambda e, hl=hl: e.tensor_copy(
                    chpp[:, hl:hl + 1], ps_v[0][:, 0:1]), deps=[op_chb])
                last_v[0] = op_chc
                ch_done[hl] = (op_chc, op_lnb, op_chb)
                prev_chb = op_chb

        for tt in range(NTT):
            par = tt % 2
            def vfn(e, tt=tt, par=par):
                for kc in range(KC):
                    i = e.matmul(ps_v[par][:, :], hsb[:, kc, tt * 128:(tt + 1) * 128],
                                 wv_sb[:, kc, :], start=(kc == 0), stop=(kc == KC - 1))
                return i
            op_vm = s.op("tensor", vfn,
                         deps=[ld_v, last_v.get(par), rope_done[-1]])
            def vefn(e, tt=tt, par=par):
                for hb in range(4):
                    i = e.tensor_copy(vsb[:, tt, hb * 65:hb * 65 + 64],
                                      ps_v[par][:, hb * 64:(hb + 1) * 64])
                return i
            op_ve = s.op("vector", vefn, deps=[op_vm, op_vones])
            last_v[par] = op_ve
            v_done.append(op_ve)
            if tt == 7:
                norm_pair(0)
            elif tt == 15:
                norm_pair(1)
        if quarter == 0:
            dbg_store("d_v", vsb[:, :, :], v_done, "(c p) e -> p c e")
            dbg_store("d_q", qrot[:, :, :], rope_done, "(c p) t -> p c t")
            dbg_store("d_k", krot[:, :, :], rope_done, "(c p) t -> p c t")
        esPq.close()

        norms_all = ch_done[3][0]
        esPa = ExitStack()
        ps_st = [esPa.enter_context(nc.psum_tensor(f"ps_st{quarter}_{i}", [128, TF], F32))
                 for i in range(4)]
        ps_cl = [esPa.enter_context(nc.psum_tensor(f"ps_cl{quarter}_{i}", [65, TF], F32))
                 for i in range(2)]
        ps_lb = esPa.enter_context(nc.psum_tensor(f"ps_lb{quarter}", [64, TF], F32))

        last_stpar = {i: norms_all for i in range(4)}
        last_pt = {}
        evac_done = [norms_all, norms_all]
        last_lb = [norms_all, norms_all]
        evac_pend = None
        for pairi in range(2):
            ml = pairi
            heads = (2 * pairi, 2 * pairi + 1)
            chcs = {hb: ch_done[2 * pairi + hb][0] for hb in range(2)}
            for qc in range(NTF):
                qbase = qc * TF
                nkt = 4 * qc + 4
                clp = qc % 2
                pend = []
                for kt in range(nkt):
                    j = kt - 4 * qc
                    off = max(0, j) * 128
                    ssl = slice(off, TF)
                    qsl = slice(qbase + off, qbase + TF)
                    prs = []
                    for hb in range(2):
                        hl = 2 * pairi + hb
                        rsl = slice(hb * 64, hb * 64 + 64)
                        spar = 2 * hb + (kt % 2)
                        ptx = 4 * hb + (kt % 4)
                        op_st = s.op("tensor", lambda e, rsl=rsl, ml=ml, kt=kt,
                                     spar=spar, ssl=ssl, qsl=qsl: e.matmul(
                            ps_st[spar][:, ssl], krot[rsl, ml, kt * 128:(kt + 1) * 128],
                            qrot[rsl, ml, qsl], start=True, stop=True),
                            deps=[last_stpar.get(spar), chcs[hb]])
                        if kt == 2 and hb == 0 and evac_pend is not None:
                            evac_pend()
                            evac_pend = None
                        op_exp = s.op("scalar", lambda e, spar=spar, ptx=ptx,
                                      ssl=ssl, hl=hl: e.activation(
                            pt_sc[:, ptx, ssl], ps_st[spar][:, ssl], AF.Exp,
                            bias=chpp[:, hl:hl + 1], scale=0.125),
                            deps=[op_st, last_pt.get(ptx)])
                        last_stpar[spar] = op_exp
                        pt_ready = op_exp
                        if j >= 0:
                            pt_ready = s.op("vector", lambda e, ptx=ptx, off=off:
                                            e.tensor_tensor(
                                pt_sc[:, ptx, off:off + 128],
                                pt_sc[:, ptx, off:off + 128],
                                tri[:, :], ALU.mult), deps=[op_exp])
                        prs.append(pt_ready)

                    def mk_lav(prs=prs, ssl=ssl, kt=kt, nkt=nkt, pairi=pairi,
                               clp=clp):
                        o_avs = []
                        for hb in range(2):
                            hl = 2 * pairi + hb
                            ptx = 4 * hb + (kt % 4)
                            o_av = s.op("tensor", lambda e, ptx=ptx, ssl=ssl,
                                        kt=kt, nkt=nkt, hl=hl, hb=hb, clp=clp:
                                        e.matmul(
                                ps_cl[hb][:, ssl], vsb[:, kt, hl * 65:(hl + 1) * 65],
                                pt_sc[:, ptx, ssl], start=(kt == 0),
                                stop=(kt == nkt - 1)),
                                deps=[prs[hb]] + ([evac_done[hb]] if kt == 0 else [])
                                     + v_done)
                            last_pt[ptx] = o_av
                            o_avs.append(o_av)
                        mk_lav.res = o_avs
                    if len(pend) >= 2:
                        pend.pop(0)()
                    pend.append(mk_lav)
                if evac_pend is not None:
                    evac_pend()
                    evac_pend = None
                for fn in pend:
                    fn()
                av_lasts = fn.res

                def mk_evac(av_lasts=av_lasts, pairi=pairi, qbase=qbase, hh0=quarter * 4 + 2 * pairi):
                    for hb in range(2):
                        hh = hh0 + hb
                        r0 = hb * 64
                        op_ln_l = s.op("scalar", lambda e, hb=hb: e.activation(
                            lrow[0:1, :], ps_cl[hb][64:65, :], AF.Ln, bias=0.0,
                            scale=1.0), deps=[av_lasts[hb], last_lb[0]])
                        op_linv = s.op("scalar", lambda e: e.activation(
                            lrow[0:1, :], lrow[0:1, :], AF.Exp, bias=0.0, scale=-1.0),
                            deps=[op_ln_l])
                        op_lb = s.op("tensor", lambda e: e.matmul(
                            ps_lb[:, :], onesrow[0:1, 0:64], lrow[0:1, :],
                            start=True, stop=True), deps=[op_linv, last_lb[1]])
                        op_lbc = s.op("vector", lambda e: e.tensor_copy(
                            lb_sb[:, :], ps_lb[:, :]), deps=[op_lb])
                        op_cev = s.op("vector", lambda e, r0=r0, hh=hh, qbase=qbase,
                                      hb=hb: e.tensor_tensor(
                            ctxT[r0:r0 + 64, hh // 2, qbase:qbase + TF],
                            ps_cl[hb][0:64, :], lb_sb[:, :], ALU.mult),
                            deps=[op_lbc, av_lasts[hb]])
                        evac_done[hb] = op_cev
                        last_lb[0] = op_lb
                        last_lb[1] = op_lbc
                        qkv_done.append(op_cev)
                evac_pend = mk_evac
        if evac_pend is not None:
            evac_pend()
            evac_pend = None
        esPa.close()
        s.barrier()
    esQ.close()
    dbg_store("d_ctx", ctxT[:, :, :], [s.last["vector"]], "(c p) t -> p c t")
    if upto <= 2:
        return _finish(nc, s, store_ops, outT, xn, dbg_out, esL, esR0,
                       (esRc, esR2, esR1))

    esRc.close()
    racc = sbL("racc", (128, KC, T), F32)
    esP3 = ExitStack()
    wo_sb = sbR(esP3, "wo_sb", (128, KC, D), BF16)
    ld_wo = s.op("gpsimd", lambda e: e.dma_start(
        wo_sb[:, :, :], woT.rearrange("(c p) e -> p c e", p=128)), dma=True, sem="lwo")
    esPo = ExitStack()
    ps_o = [esPo.enter_context(nc.psum_tensor(f"ps_o{i}", [128, TF], F32))
            for i in range(2)]
    last_o = {}
    racc_done = []
    for tf in range(NTF):
        sl = slice(tf * TF, (tf + 1) * TF)
        for mc in range(KC):
            par = (mc * NTF + tf) % 2
            def ofn(e, mc=mc, sl=sl, par=par):
                for kc in range(KC):
                    i = e.matmul(ps_o[par][:, :], wo_sb[:, kc, mc * 128:(mc + 1) * 128],
                                 ctxT[:, kc, sl], start=(kc == 0), stop=(kc == KC - 1))
                return i
            op_om = s.op("tensor", ofn, deps=[ld_wo, last_o.get(par)])
            op_oe = s.op("vector", lambda e, mc=mc, sl=sl, par=par: e.tensor_tensor(
                racc[:, mc, sl], ps_o[par][:, :], xn[:, mc, sl], ALU.add),
                deps=[op_om])
            last_o[par] = op_oe
            racc_done.append(op_oe)

    dbg_store("d_racc", racc[:, :, :], racc_done, "(c p) t -> p c t")
    esPo.close()
    s.barrier()
    esP3.close()
    esR2.close()
    if upto <= 3:
        return _finish(nc, s, store_ops, outT, racc, dbg_out, esL, esR0,
                       (esR2, esR1))

    op_abs = s.op("scalar", lambda e: e.activation(srow[0:1, 2:4], scl[0:1, 0:2],
                  AF.Abs, bias=0.0, scale=1.0), deps=ld_consts)
    op_lns = s.op("scalar", lambda e: e.activation(srow[0:1, 2:4], srow[0:1, 2:4],
                  AF.Ln, bias=0.0, scale=1.0), deps=[op_abs])
    op_exs = s.op("scalar", lambda e: e.activation(srow[0:1, 2:4], srow[0:1, 2:4],
                  AF.Exp, bias=0.0, scale=-1.0), deps=[op_lns])
    op_sgn = s.op("scalar", lambda e: e.activation(srow[0:1, 6:8], scl[0:1, 0:2],
                  AF.Sign, bias=0.0, scale=1.0), deps=[op_exs])
    op_rr = s.op("vector", lambda e: e.tensor_tensor(
        srow[0:1, 2:4], srow[0:1, 2:4], srow[0:1, 6:8], ALU.mult), deps=[op_sgn])
    op_sp = s.op("vector", lambda e: e.tensor_tensor(
        srow[0:1, 4:5], scl[0:1, 0:1], scl[0:1, 1:2], ALU.mult), deps=[op_rr])
    esPsc = ExitStack()
    ps_sc = esPsc.enter_context(nc.psum_tensor("ps_sc", [128, 4], F32))
    op_spm = s.op("tensor", lambda e: e.matmul(
        ps_sc[:, 0:3], onesrow[0:1, :], srow[0:1, 2:5],
        start=True, stop=True), deps=[op_sp])
    op_spc = s.op("vector", lambda e: e.tensor_copy(spp[:, 0:3], ps_sc[:, 0:3]),
                  deps=[op_spm])
    esPsc.close()
    xnq_done = []
    for mc in range(KC):
        op_q8 = s.op("scalar", lambda e, mc=mc: e.activation(
            xnq[:, mc, :], xn[:, mc, :], AF.Copy, bias=0.0, scale=spp[:, 0:1]),
            deps=[op_spc])
        xnq_done.append(op_q8)
    esR1.close()

    esP5 = ExitStack()
    NSLOT = 4
    slots = [sbR(esP5, f"wslot{i}", (128, KC, 1024), BF16) for i in range(NSLOT)]
    mats = [(sgT, None), (suT, None), (sdT, None)]
    for e_i in range(E):
        mats += [(egT, e_i), (euT, e_i), (edT, e_i)]
    dma_ev = [None] * len(mats)
    cons_ev = [None] * len(mats)

    def append_dma(idx):
        src_, e_i = mats[idx]
        slot = slots[idx % NSLOT]
        if e_i is None:
            apsrc = src_.rearrange("(c p) n -> p c n", p=128)
        else:
            apsrc = src_[e_i].rearrange("(c p) n -> p c n", p=128)
        dma_ev[idx] = s.op("gpsimd", lambda e, t=slot, a=apsrc: e.dma_start(
            t[:, :, :], a),
            deps=[cons_ev[idx - NSLOT]] if idx >= NSLOT else [xnq_done[-1]],
            dma=True, sem=f"ls{idx % 4}")

    for _i in range(NSLOT):
        dma_ev[_i] = None
    def _start_dmas():
        for _i in range(NSLOT):
            append_dma(_i)
    _sd_done = False
    _start_dmas()
    h2 = sbL("h2", (128, KC, T), BF16)
    econ = sbL("econ", (16, 16 * 128), BF16)
    cwT = sbL("cwT", (16, T), BF16)
    ebias_bc = sbL("ebias_bc", (128, E), F32)
    ld_econ = s.op("sync", lambda e: e.dma_start(econ[:, :], econ_h), dma=True,
                   sem="lde")
    esP4 = ExitStack()
    rows2 = sbR(esP4, "rows2", (1, 1, T), F32)
    rsq_sc4 = sbR(esP4, "rsq_sc4", (128, 2, TF), BF16)
    r2bc = sbR(esP4, "r2bc", (128, T), F32)
    gsc = sbR(esP4, "gsc", (128, 4, E), F32)
    cwbf = sbR(esP4, "cwbf", (128, E), BF16)
    tcol = sbR(esP4, "tcol", (128, 4), F32)
    esPs4 = ExitStack()
    ps_sq2 = esPs4.enter_context(nc.psum_tensor("ps_sq2", [1, T], F32))
    last_rsq = {}
    sq2_ops = []
    stat_pend = []
    for tf in range(NTF):
        sl = slice(tf * TF, (tf + 1) * TF)
        for mc in range(KC):
            def mk_stat(mc=mc, sl=sl):
                spar = mc % 2
                op_r2 = s.op("vector", lambda e, mc=mc, sl=sl, spar=spar:
                             e.tensor_tensor(
                    rsq_sc4[:, spar, :], racc[:, mc, sl], racc[:, mc, sl], ALU.mult),
                    deps=[last_rsq.get(spar)])
                op_sm = s.op("tensor", lambda e, sl=sl, spar=spar, mc=mc: e.matmul(
                    ps_sq2[0:1, sl], onescol[:, 0:1], rsq_sc4[:, spar, :],
                    start=(mc == 0), stop=(mc == KC - 1)), deps=[op_r2])
                last_rsq[spar] = op_sm
                if mc == KC - 1:
                    sq2_ops.append(op_sm)
            if len(stat_pend) >= 2:
                stat_pend.pop(0)()
            stat_pend.append(mk_stat)
    for fn in stat_pend:
        fn()
    op_ms2 = s.op("scalar", lambda e: e.activation(rows2[0:1, 0, :], ps_sq2[0:1, :],
                  AF.Copy, bias=0.0, scale=1.0 / D), deps=sq2_ops)
    op_ln2 = s.op("scalar", lambda e: e.activation(rows2[0:1, 0, :], rows2[0:1, 0, :],
                  AF.Ln, bias=miscrow[0:1, 1:2], scale=1.0), deps=[op_ms2])
    op_rm2 = s.op("scalar", lambda e: e.activation(rows2[0:1, 0, :], rows2[0:1, 0, :],
                  AF.Exp, bias=0.0, scale=-0.5), deps=[op_ln2])
    esPs4.close()
    es_bc2 = ExitStack()
    ps_bc2 = [es_bc2.enter_context(nc.psum_tensor(f"ps_bc2{i}", [128, TF], F32))
              for i in range(2)]
    bc2_done = []
    last_bc2 = {}
    for tf in range(NTF):
        sl = slice(tf * TF, (tf + 1) * TF)
        par = tf % 2
        op_bm = s.op("tensor", lambda e, sl=sl, par=par: e.matmul(
            ps_bc2[par][:, :], onesrow[0:1, :], rows2[0:1, 0, sl], start=True, stop=True),
            deps=[op_rm2, last_bc2.get(par)])
        op_bc = s.op("vector", lambda e, sl=sl, par=par: e.tensor_copy(
            r2bc[:, sl], ps_bc2[par][:, :]), deps=[op_bm])
        last_bc2[par] = op_bc
        bc2_done.append(op_bc)
    h2_done = []
    for tf in range(NTF):
        sl = slice(tf * TF, (tf + 1) * TF)
        for mc in range(KC):
            op_h2 = s.op("vector", lambda e, mc=mc, sl=sl: e.scalar_tensor_tensor(
                h2[:, mc, sl], r2bc[:, sl], n2pp[:, mc:mc + 1], racc[:, mc, sl],
                ALU.mult, ALU.mult), deps=bc2_done)
            h2_done.append(op_h2)
    dbg_store("d_h2", h2[:, :, :], h2_done, "(c p) t -> p c t")
    es_bc2.close()

    rw_sb = sbR(esP4, "rw_sb", (128, KC, E), BF16)
    ld_rw = s.op("gpsimd", lambda e: e.dma_start(
        rw_sb[:, :, :], rwT.rearrange("(c p) e -> p c e", p=128)), dma=True, sem="lwr")
    esPr = ExitStack()
    ps_lg = [esPr.enter_context(nc.psum_tensor(f"ps_lg{i}", [128, E], F32))
             for i in range(2)]
    ps_tr = esPr.enter_context(nc.psum_tensor("ps_tr", [16, 128], BF16))
    op_ebm = s.op("tensor", lambda e: e.matmul(
        ps_lg[0][:, :], onesrow[0:1, :], ebias[0:1, :], start=True, stop=True))
    op_ebc = s.op("vector", lambda e: e.tensor_copy(ebias_bc[:, :], ps_lg[0][:, :]),
                  deps=[op_ebm])
    last_lg = {0: op_ebc}
    last_tr = None
    prev_gg = prev_de = None
    cw_done = []
    for tt in range(NTT):
        par = tt % 2
        def lgfn(e, tt=tt, par=par):
            for kc in range(KC):
                i = e.matmul(ps_lg[par][:, :], h2[:, kc, tt * 128:(tt + 1) * 128],
                             rw_sb[:, kc, :], start=(kc == 0), stop=(kc == KC - 1))
            return i
        op_lg = s.op("tensor", lgfn,
                     deps=[ld_rw, last_lg.get(par)]
                          + h2_done[8 * (tt // 4):8 * (tt // 4) + 8])
        op_ga = s.op("vector", lambda e, par=par: e.tensor_tensor(
            gsc[:, 0, :], ps_lg[par][:, :], ebias_bc[:, :], ALU.add),
            deps=[op_lg, op_ebc, prev_gg, prev_de])
        last_lg[par] = op_ga
        op_gx = s.op("scalar", lambda e: e.activation(gsc[:, 1, :], gsc[:, 0, :],
                     AF.Exp, bias=0.0, scale=-1.0), deps=[op_ga])
        op_gl = s.op("scalar", lambda e: e.activation(gsc[:, 1, :], gsc[:, 1, :],
                     AF.Ln, bias=onecolf[:, 0:1], scale=1.0), deps=[op_gx, op_onesf])
        op_gg = s.op("scalar", lambda e: e.activation(gsc[:, 0, :], gsc[:, 1, :],
                     AF.Exp, bias=0.0, scale=-1.0), deps=[op_gl])
        op_m1 = s.op("vector", lambda e: e.tensor_reduce(
            tcol[:, 0:1], gsc[:, 0, :], AX.X, ALU.max), deps=[op_gg])
        op_i1 = s.op("vector", lambda e: e.tensor_scalar(
            gsc[:, 1, :], gsc[:, 0, :], tcol[:, 0:1], None, ALU.is_ge), deps=[op_m1])
        op_g2 = s.op("vector", lambda e: e.scalar_tensor_tensor(
            gsc[:, 2, :], gsc[:, 1, :], -1e9, gsc[:, 0, :], ALU.mult, ALU.add),
            deps=[op_i1])
        op_m2 = s.op("vector", lambda e: e.tensor_reduce(
            tcol[:, 1:2], gsc[:, 2, :], AX.X, ALU.max), deps=[op_g2])
        op_i2 = s.op("vector", lambda e: e.tensor_scalar(
            gsc[:, 2, :], gsc[:, 2, :], tcol[:, 1:2], None, ALU.is_ge), deps=[op_m2])
        op_dn = s.op("vector", lambda e: e.tensor_tensor(
            tcol[:, 2:3], tcol[:, 0:1], tcol[:, 1:2], ALU.add), deps=[op_m2])
        op_dn2 = s.op("vector", lambda e: e.tensor_scalar(
            tcol[:, 2:3], tcol[:, 2:3], 1e-9, None, ALU.add), deps=[op_dn])
        op_dl = s.op("scalar", lambda e: e.activation(tcol[:, 2:3], tcol[:, 2:3],
                     AF.Ln, bias=0.0, scale=1.0), deps=[op_dn2])
        op_de = s.op("scalar", lambda e: e.activation(tcol[:, 2:3], tcol[:, 2:3],
                     AF.Exp, bias=0.0, scale=-1.0), deps=[op_dl])
        op_w1 = s.op("vector", lambda e: e.tensor_tensor(
            tcol[:, 0:1], tcol[:, 0:1], tcol[:, 2:3], ALU.mult), deps=[op_de])
        op_w2 = s.op("vector", lambda e: e.tensor_tensor(
            tcol[:, 1:2], tcol[:, 1:2], tcol[:, 2:3], ALU.mult), deps=[op_de])
        op_cw1 = s.op("vector", lambda e: e.tensor_scalar(
            gsc[:, 3, :], gsc[:, 1, :], tcol[:, 0:1], None, ALU.mult), deps=[op_w1, op_i2])
        op_cw = s.op("vector", lambda e: e.scalar_tensor_tensor(
            cwbf[:, :], gsc[:, 2, :], tcol[:, 1:2], gsc[:, 3, :], ALU.mult, ALU.add),
            deps=[op_cw1, op_w2])
        op_tr = s.op("tensor", lambda e: e.transpose(
            ps_tr[:, :], cwbf[:, :], ident[:, :]), deps=[op_cw, last_tr])
        op_tc = s.op("vector", lambda e, tt=tt: e.tensor_copy(
            cwT[0:16, tt * 128:(tt + 1) * 128], ps_tr[:, :]), deps=[op_tr])
        last_tr = op_tc
        prev_gg, prev_de = op_gg, op_de
        cw_done.append(op_tc)
    dbg_store("d_cw", cwT[:, :], cw_done)
    esPr.close()

    s.barrier()
    esP4.close()
    if upto <= 4:
        return _finish(nc, s, store_ops, outT, racc, dbg_out, esL, esR0,
                       (esR1,))

    z_sb = sbR(esP5, "z_sb", (128, KC, TF), BF16)
    wbc = sbR(esP5, "wbc", (128, T), BF16)
    zsil = sbR(esP5, "zsil", (128, 2, TF), BF16)
    tz = sbR(esP5, "tz", (128, 2, TF), F32)

    esPm = ExitStack()
    ps_g = [esPm.enter_context(nc.psum_tensor(f"ps_g{i}", [128, TF], F32)) for i in range(2)]
    ps_u = [esPm.enter_context(nc.psum_tensor(f"ps_u{i}", [128, TF], F32)) for i in range(2)]
    ps_y = [esPm.enter_context(nc.psum_tensor(f"ps_y{i}", [128, TF], F32)) for i in range(2)]
    ps_wb = esPm.enter_context(nc.psum_tensor("ps_wb", [128, TF], F32))

    last_gp = {}
    last_up = {}
    last_yp = {}
    last_wbc = None
    last_zsil = {}
    last_tz = {}
    moe_done = []
    for set_i in range(17):
        e_i = set_i - 1
        ig, iu, idn = 3 * set_i, 3 * set_i + 1, 3 * set_i + 2
        sg_slot, su_slot, sd_slot = (slots[ig % NSLOT], slots[iu % NSLOT],
                                     slots[idn % NSLOT])
        wbc_ops = []
        if e_i >= 0:
            for tf in range(NTF):
                sl = slice(tf * TF, (tf + 1) * TF)
                op_wm = s.op("tensor", lambda e, e_i=e_i, sl=sl: e.matmul(
                    ps_wb[:, :], econ[0:16, e_i * 128:(e_i + 1) * 128], cwT[0:16, sl],
                    start=True, stop=True), deps=cw_done + [last_wbc, ld_econ])
                op_wc = s.op("vector", lambda e, sl=sl: e.tensor_copy(
                    wbc[:, sl], ps_wb[:, :]), deps=[op_wm])
                last_wbc = op_wc
                wbc_ops.append(op_wc)
        g_cons = u_cons = d_cons = None
        for tf in range(NTF):
            sl = slice(tf * TF, (tf + 1) * TF)
            z_ready = []
            for mc in range(KC):
                par = mc % 2
                def gfn(e, mc=mc, sl=sl, par=par, w=sg_slot):
                    for kc in range(KC):
                        i = e.matmul(ps_g[par][:, :], w[:, kc, mc * 128:(mc + 1) * 128],
                                     h2[:, kc, sl], start=(kc == 0), stop=(kc == KC - 1))
                    return i
                op_gm = s.op("tensor", gfn, deps=[dma_ev[ig], last_gp.get(par)])
                g_cons = op_gm
                op_sl = s.op("scalar", lambda e, par=par: e.activation(
                    zsil[:, par, :], ps_g[par][:, :], AF.Silu, bias=0.0, scale=1.0),
                    deps=[op_gm, last_zsil.get(par)])
                last_gp[par] = op_sl
                def ufn(e, mc=mc, sl=sl, par=par, w=su_slot):
                    for kc in range(KC):
                        i = e.matmul(ps_u[par][:, :], w[:, kc, mc * 128:(mc + 1) * 128],
                                     h2[:, kc, sl], start=(kc == 0), stop=(kc == KC - 1))
                    return i
                op_um = s.op("tensor", ufn, deps=[dma_ev[iu], last_up.get(par)])
                u_cons = op_um
                if e_i >= 0:
                    op_t = s.op("vector", lambda e, par=par: e.tensor_tensor(
                        tz[:, par, :], ps_u[par][:, :], zsil[:, par, :], ALU.mult),
                        deps=[op_um, op_sl, last_tz.get(par)])
                    last_up[par] = op_t
                    op_z = s.op("vector", lambda e, mc=mc, par=par, sl=sl:
                                e.tensor_tensor(z_sb[:, mc, :], tz[:, par, :],
                                                wbc[:, sl], ALU.mult),
                                deps=[op_t] + wbc_ops)
                    last_tz[par] = op_z
                else:
                    op_z = s.op("vector", lambda e, mc=mc, par=par: e.tensor_tensor(
                        z_sb[:, mc, :], ps_u[par][:, :], zsil[:, par, :], ALU.mult),
                        deps=[op_um, op_sl])
                    last_up[par] = op_z
                last_zsil[par] = op_z
                z_ready.append(op_z)
            for dc in range(KC):
                par = dc % 2
                def yfn1(e, dc=dc, par=par, w=sd_slot):
                    for ic in range(4):
                        i = e.matmul(ps_y[par][:, :], w[:, ic, dc * 128:(dc + 1) * 128],
                                     z_sb[:, ic, :], start=(ic == 0), stop=False)
                    return i
                def yfn2(e, dc=dc, par=par, w=sd_slot):
                    for ic in range(4, KC):
                        i = e.matmul(ps_y[par][:, :], w[:, ic, dc * 128:(dc + 1) * 128],
                                     z_sb[:, ic, :], start=False, stop=(ic == KC - 1))
                    return i
                s.op("tensor", yfn1, deps=z_ready[:4] + [dma_ev[idn],
                     last_yp.get(par)])
                op_ym = s.op("tensor", yfn2, deps=z_ready[4:])
                d_cons = op_ym
                op_ra = s.op("vector", lambda e, dc=dc, sl=sl, par=par: e.tensor_tensor(
                    racc[:, dc, sl], ps_y[par][:, :], racc[:, dc, sl], ALU.add),
                    deps=[op_ym])
                last_yp[par] = op_ra
                moe_done.append(op_ra)
        cons_ev[ig], cons_ev[iu], cons_ev[idn] = g_cons, u_cons, d_cons
        for j in range(3 * set_i + NSLOT, min(3 * set_i + 3 + NSLOT, len(mats))):
            if dma_ev[j] is None:
                append_dma(j)
    dbg_store("d_s", racc[:, :, :], moe_done[-4:], "(c p) t -> p c t")
    esPm.close()
    s.barrier()
    esP5.close()
    if upto <= 5:
        return _finish(nc, s, store_ops, outT, racc, dbg_out, esL, esR0)

    esR6 = ExitStack()
    g_sb = sbR(esR6, "g_sb", (128, KC, T), BF16)
    esP6 = ExitStack()
    kq = sbR(esP6, "kq", (128, KC, 1024), F8)
    fk_sb = sbR(esP6, "fk_sb", (128, KC, 1024), BF16)
    y2_sc = sbR(esP6, "y2_sc", (128, 2, TF), BF16)
    ld_fk = s.op("sync", lambda e: e.dma_start(
        fk_sb[:, :, :], ffnK.rearrange("(c p) f -> p c f", p=128)), dma=True, sem="ldf")
    kq_done = []
    for mc in range(KC):
        op_kq = s.op("scalar", lambda e, mc=mc: e.activation(
            kq[:, mc, :], fk_sb[:, mc, :], AF.Copy, bias=0.0, scale=spp[:, 1:2]),
            deps=[ld_fk, op_spc])
        kq_done.append(op_kq)
    esPf = ExitStack()
    ps_f = [esPf.enter_context(nc.psum_tensor(f"ps_f{i}", [128, TF], F32))
            for i in range(2)]
    last_f = {}
    g_done = []
    for fc in range(KC):
        for tf in range(NTF):
            sl = slice(tf * TF, (tf + 1) * TF)
            par = (fc * NTF + tf) % 2
            def ffn_mm(e, fc=fc, sl=sl, par=par):
                for kc in range(KC):
                    i = e.matmul(ps_f[par][:, :], kq[:, kc, fc * 128:(fc + 1) * 128],
                                 xnq[:, kc, sl], start=(kc == 0), stop=(kc == KC - 1))
                return i
            op_fm = s.op("tensor", ffn_mm, deps=kq_done + xnq_done + [last_f.get(par)])
            op_y2 = s.op("vector", lambda e, fc=fc, par=par: e.tensor_scalar(
                y2_sc[:, par, :], ps_f[par][:, :], spp[:, 2:3], biaspp[:, fc:fc + 1],
                ALU.mult, ALU.add), deps=[op_fm])
            last_f[par] = op_y2
            op_gs = s.op("scalar", lambda e, fc=fc, sl=sl, par=par: e.activation(
                g_sb[:, fc, sl], y2_sc[:, par, :], AF.Silu, bias=0.0, scale=1.0),
                deps=[op_y2])
            g_done.append(op_gs)
    dbg_store("d_g", g_sb[:, :, :], g_done, "(c p) t -> p c t")
    last_ffn_mm = last_f
    esPf.close()
    esP6.close()

    esP7 = ExitStack()
    xf32 = sbR(esP7, "xf32", (128, 2, T), F32)
    tmp7 = sbR(esP7, "tmp7", (128, T), F32)
    osb = sbR(esP7, "osb", (128, 2, T), F32)
    last_x = {}
    last_os = {}
    last_tmp = None
    ffn_pe_done = [last_ffn_mm.get(0), last_ffn_mm.get(1)]
    for mc in range(KC):
        par = mc % 2
        op_lx = s.op("sync", lambda e, mc=mc, par=par: e.dma_start(
            xf32[:, par, :], xT[mc * 128:(mc + 1) * 128, :]),
            deps=[last_x.get(par)] + ffn_pe_done, dma=True, sem=f"lx{par}")
        op_sg = s.op("vector", lambda e, mc=mc: e.tensor_tensor(
            tmp7[:, :], racc[:, mc, :], g_sb[:, mc, :], ALU.mult),
            deps=[last_tmp, g_done[-1], moe_done[-1]])
        op_out = s.op("vector", lambda e, mc=mc, par=par: e.tensor_tensor(
            osb[:, par, :], tmp7[:, :], xf32[:, par, :], ALU.add),
            deps=[op_sg, op_lx, last_os.get(par)])
        last_tmp = op_out
        op_st = s.op("sync", lambda e, mc=mc, par=par: e.dma_start(
            outT[mc * 128:(mc + 1) * 128, :], osb[:, par, :]),
            deps=[op_out], dma=True, sem=f"st{par}")
        last_x[par] = op_out
        last_os[par] = op_st
        store_ops.append(op_st)
    esP7.close()
    esR6.close()
    return _finish(nc, s, store_ops, outT, racc, dbg_out, esL, esR0)


def _finish(nc, s, store_ops, outT, racc, dbg_out, esL, esR0, extra=()):
    if not any(o.sem.startswith(("lds", "st")) for o in store_ops):
        op = s.op("gpsimd", lambda e: e.dma_start(
            outT.rearrange("(c p) t -> p c t", p=128), racc[:, :, :]),
            deps=[s.last.get("vector"), s.last.get("scalar"), s.last.get("tensor")],
            dma=True)
        store_ops.append(op)
    s.wait_only("sync", store_ops)
    s.wait_only("gpsimd", [o for o in store_ops if o.sem == "ldg"])

    with ExitStack() as es:
        sems = {}
        sem_names = set()
        for o in s.ops:
            if o.sem:
                sem_names.add(o.sem)
        for o in s.ops:
            for d in o.deps:
                sem_names.add(d.sem)
        for name in sorted(sem_names):
            sems[name] = es.enter_context(nc.semaphore(name))
        block = es.enter_context(nc.Block())

        @block.tensor
        def _(eng):
            s.emit("tensor", eng, sems)

        @block.vector
        def _(eng):
            s.emit("vector", eng, sems)

        @block.scalar
        def _(eng):
            s.emit("scalar", eng, sems)

        @block.gpsimd
        def _(eng):
            s.emit("gpsimd", eng, sems)

        @block.sync
        def _(eng):
            s.emit("sync", eng, sems)
    for es_x in extra:
        try:
            es_x.close()
        except Exception:
            pass
    esR0.close()
    esL.close()
    return nc


def make_inputs(inputs, consts=None):
    f32 = np.float32
    c = consts or _host_consts()
    x = np.asarray(inputs["L0_attn_x"])
    xTs = [np.ascontiguousarray(x[b].T).astype(f32, copy=False) for b in range(x.shape[0])]
    base = {
        "wqT": np.ascontiguousarray(np.asarray(inputs["L0_attn_wq"]).T),
        "wkT": np.ascontiguousarray(np.asarray(inputs["L0_attn_wk"]).T),
        "wvT": np.ascontiguousarray(np.asarray(inputs["L0_attn_wv"]).T),
        "woT": np.ascontiguousarray(np.asarray(inputs["L0_attn_wo"]).T),
        "sgT": np.ascontiguousarray(np.asarray(inputs["L0_attn_sg"]).T),
        "suT": np.ascontiguousarray(np.asarray(inputs["L0_attn_su"]).T),
        "sdT": np.ascontiguousarray(np.asarray(inputs["L0_attn_sd"]).T),
        "egT": np.ascontiguousarray(np.asarray(inputs["L0_attn_eg"]).transpose(0, 2, 1)),
        "euT": np.ascontiguousarray(np.asarray(inputs["L0_attn_eu"]).transpose(0, 2, 1)),
        "edT": np.ascontiguousarray(np.asarray(inputs["L0_attn_ed"]).transpose(0, 2, 1)),
        "rwT": np.ascontiguousarray(np.asarray(inputs["L0_attn_router_w"]).T),
        "ffnK": np.ascontiguousarray(np.asarray(inputs["L0_ffn_kernel"])),
        "n1pp_h": np.ascontiguousarray(
            np.asarray(inputs["L0_attn_norm1_w"]).reshape(KC, 128).T).astype(f32),
        "n2pp_h": np.ascontiguousarray(
            np.asarray(inputs["L0_attn_norm2_w"]).reshape(KC, 128).T).astype(f32),
        "biaspp_h": np.ascontiguousarray(
            np.asarray(inputs["L0_ffn_bias"]).astype(f32).reshape(KC, 128).T),
        "ebias_h": np.asarray(inputs["L0_attn_expert_bias"]).reshape(1, E).astype(f32),
        "scl_h": np.stack([np.asarray(inputs["L0_ffn_input_scale"]).reshape(()),
                         np.asarray(inputs["L0_ffn_kernel_scale"]).reshape(())]
                        ).reshape(1, 2).astype(f32),
        "cosT_h": c["cosT"], "sinT_h": c["sinT"], "mrot_h": c["mrot"], "tri_h": c["tri"],
        "ident_h": c["ident"], "econ_h": c["econ"], "onesrow_h": c["onesrow"],
        "negrow_h": c["negrow"],
        "miscrow_h": np.array([[1e-5, 1e-6, 0.0, 0.0]], np.float32),
    }
    in_maps = []
    for b in range(len(xTs)):
        m = dict(base)
        m["xT"] = xTs[b]
        in_maps.append(m)
    return in_maps


def kernel(**inputs):
    nc = build_nc(upto=7)
    in_maps = make_inputs(inputs)
    res = run_bass_kernel_spmd(nc, in_maps, core_ids=list(range(NCORES)))
    out = np.stack([np.ascontiguousarray(r["outT"].T) for r in res.results], axis=0)
    return out.astype(np.float32)


# version 19
# speedup vs baseline: 1.0019x; 1.0019x over previous
import numpy as np
import ml_dtypes
from contextlib import ExitStack

import concourse.bass as bass
import concourse.mybir as mybir
from concourse.bass_utils import run_bass_kernel_spmd

F32 = mybir.dt.float32
BF16 = mybir.dt.bfloat16
F8 = mybir.dt.float8e4
AF = mybir.ActivationFunctionType
ALU = mybir.AluOpType
AX = mybir.AxisListType

T, D, H, DH, E, NI = 2048, 1024, 16, 64, 16, 1024
KC = 8
NTF = 4
TF = 512
NTT = 16
NCORES = 8

_SEM_OF_ENGINE = {"tensor": "pe", "vector": "dv", "scalar": "ac"}


class _Op:
    __slots__ = ("engine", "fn", "deps", "sem", "val", "inc")

    def __init__(self, engine, fn, deps, sem, val, inc):
        self.engine, self.fn, self.deps = engine, fn, deps
        self.sem, self.val, self.inc = sem, val, inc


class Sched:

    def __init__(self):
        self.ops = []
        self.counts = {}
        self.bar = []
        self.last = {}

    def op(self, engine, fn, deps=(), dma=False, sem=None):
        if dma:
            sem = sem or ("ldg" if engine == "gpsimd" else "lds")
            inc = 16
        else:
            sem = _SEM_OF_ENGINE[engine]
            inc = 1
        c = self.counts.get(sem, 0) + inc
        self.counts[sem] = c
        dd = [d for d in deps if d is not None] + self.bar
        o = _Op(engine, fn, dd, sem, c, inc)
        self.ops.append(o)
        self.last[engine] = o
        return o

    def wait_only(self, engine, deps):
        o = _Op(engine, None, [d for d in deps if d is not None], None, None, 0)
        self.ops.append(o)
        return o

    def barrier(self):
        self.bar = list(self.last.values())

    def emit(self, engine_name, eng, sems):
        hw = {}
        for o in self.ops:
            if o.engine != engine_name:
                continue
            for d in o.deps:
                if hw.get(d.sem, 0) < d.val:
                    eng.wait_ge(sems[d.sem], d.val)
                    hw[d.sem] = d.val
            if o.fn is not None:
                last = o.fn(eng)
                last.then_inc(sems[o.sem], o.inc)


def _host_consts():
    bf = ml_dtypes.bfloat16
    inv = 1.0 / 10000.0 ** (np.arange(0, DH, 2, dtype=np.float64) / DH)
    fr = np.arange(T, dtype=np.float64)[:, None] * inv[None, :]
    emb = np.concatenate([fr, fr], -1)
    cosT = np.tile(np.cos(emb).T.astype(np.float32), (2, 1))
    sinT = np.tile(np.sin(emb).T.astype(np.float32), (2, 1))
    mrot = np.zeros((128, 128), np.float32)
    for m in range(128):
        if m % 64 < 32:
            mrot[(m + 32), m] = -1.0
        else:
            mrot[(m - 32), m] = 1.0
    tri = np.triu(np.ones((128, 128), np.float32))
    ident = np.eye(128, dtype=np.float32)
    econ = np.zeros((16, 16 * 128), np.float32)
    for e in range(16):
        econ[e, e * 128:(e + 1) * 128] = 1.0
    return {
        "cosT": cosT, "sinT": sinT,
        "mrot": mrot.astype(bf), "tri": tri.astype(bf), "ident": ident.astype(bf),
        "econ": econ.astype(bf),
        "onesrow": np.ones((1, 128), np.float32),
        "negrow": -np.ones((1, 128), np.float32),
    }


def build_nc(upto=7, dbg=()):
    nc = bass.Bass("TRN2", target_bir_lowering=False)

    def din(name, shape, dt=F32):
        return nc.dram_tensor(name, list(shape), dt, kind="ExternalInput").ap()

    xT = din("xT", (D, T))
    wqT = din("wqT", (D, D)); wkT = din("wkT", (D, D)); wvT = din("wvT", (D, D))
    woT = din("woT", (D, D))
    sgT = din("sgT", (D, NI)); suT = din("suT", (D, NI)); sdT = din("sdT", (NI, D))
    egT = din("egT", (E, D, NI)); euT = din("euT", (E, D, NI)); edT = din("edT", (E, NI, D))
    rwT = din("rwT", (D, E))
    ffnK = din("ffnK", (D, D), BF16)
    n1pp_h = din("n1pp_h", (128, KC)); n2pp_h = din("n2pp_h", (128, KC))
    biaspp_h = din("biaspp_h", (128, KC))
    ebias_h = din("ebias_h", (1, E))
    scl_h = din("scl_h", (1, 2))
    cosT_h = din("cosT_h", (128, T)); sinT_h = din("sinT_h", (128, T))
    mrot_h = din("mrot_h", (128, 128), BF16); tri_h = din("tri_h", (128, 128), BF16)
    ident_h = din("ident_h", (128, 128), BF16); econ_h = din("econ_h", (16, 16 * 128), BF16)
    onesrow_h = din("onesrow_h", (1, 128)); negrow_h = din("negrow_h", (1, 128))
    miscrow_h = din("miscrow_h", (1, 4))

    outT = nc.dram_tensor("outT", [D, T], F32, kind="ExternalOutput").ap()
    dbg_out = {}
    for name, shape, dt in (
        ("d_xn", (D, T), BF16), ("d_h", (D, T), BF16), ("d_q", (256, T), BF16),
        ("d_k", (256, T), BF16), ("d_v", (T, 260), BF16), ("d_ctx", (D, T), BF16),
        ("d_racc", (D, T), F32), ("d_h2", (D, T), BF16), ("d_cw", (E, T), BF16),
        ("d_g", (D, T), BF16), ("d_s", (D, T), F32),
    ):
        if name in dbg:
            dbg_out[name] = nc.dram_tensor(name, list(shape), dt, kind="ExternalOutput").ap()

    s = Sched()
    store_ops = []

    def dbg_store(name, sb_ap, deps, rearr=None):
        if name in dbg_out:
            tgt = dbg_out[name]
            if rearr is not None:
                tgt = tgt.rearrange(rearr, p=128)
            o = s.op("sync", lambda e, a=sb_ap, t=tgt: e.dma_start(t, a),
                     deps=list(deps), dma=True)
            store_ops.append(o)

    esL = ExitStack()
    esR0 = ExitStack()

    def sbL(name, shape, dt):
        return esL.enter_context(nc.sbuf_tensor(name, list(shape), dt, side="left"))

    def sbR(es, name, shape, dt):
        return es.enter_context(nc.sbuf_tensor(name, list(shape), dt, side="right"))

    mrot = sbL("mrot", (128, 128), BF16)
    tri = sbL("tri", (128, 128), BF16)
    ident = sbL("ident", (128, 128), BF16)
    onescol = sbL("onescol", (128, 1), BF16)
    onesrow = sbL("onesrow", (1, 128), F32)
    negrow = sbL("negrow", (1, 128), F32)
    n1pp = sbL("n1pp", (128, KC), F32)
    n2pp = sbL("n2pp", (128, KC), F32)
    biaspp = sbL("biaspp", (128, KC), F32)
    ebias = sbL("ebias", (1, E), F32)
    scl = sbL("scl", (1, 2), F32)
    srow = sbL("srow", (1, 8), F32)
    miscrow = sbL("miscrow", (1, 4), F32)
    onecolf = sbL("onecolf", (128, 1), F32)
    spp = sbL("spp", (128, 4), F32)

    esR1 = ExitStack()
    esR2 = ExitStack()
    esRc = ExitStack()
    xnq = sbR(esR0, "xnq", (128, KC, T), F8)
    xn = sbR(esR1, "xn", (128, KC, T), BF16)
    hsb = sbR(esR2, "hsb", (128, KC, T), BF16)
    ctxT = sbR(esR2, "ctxT", (128, KC, T), BF16)
    cos_sb = sbR(esRc, "cos_sb", (128, T), F32)
    sin_sb = sbR(esRc, "sin_sb", (128, T), F32)

    ld_consts = []
    for ap_h, sb in ((cosT_h, cos_sb), (sinT_h, sin_sb), (mrot_h, mrot), (tri_h, tri),
                     (ident_h, ident), (onesrow_h, onesrow),
                     (negrow_h, negrow), (n1pp_h, n1pp), (n2pp_h, n2pp),
                     (biaspp_h, biaspp), (ebias_h, ebias), (scl_h, scl),
                     (miscrow_h, miscrow)):
        ld_consts.append(s.op("sync", lambda e, a=ap_h, t=sb: e.dma_start(t[:, :], a),
                              dma=True, sem="ldc"))
    op_ones = s.op("vector", lambda e: e.memset(onescol[:, :], 1.0))
    op_onesf = s.op("vector", lambda e: e.memset(onecolf[:, :], 1.0))
    s.bar = [ld_consts[-1], op_onesf]

    esP1 = ExitStack()
    xbf = sbR(esP1, "xbf", (128, KC, T), BF16)
    rows = sbR(esP1, "rows", (1, 4, T), F32)
    bct = sbR(esP1, "bct", (128, 3, TF), F32)
    sq_sc = sbR(esP1, "sq_sc", (128, 2, TF), BF16)
    t1_sc = sbR(esP1, "t1_sc", (128, 2, TF), F32)

    op_ldx = s.op("gpsimd", lambda e: e.dma_start(
        xbf[:, :, :], xT.rearrange("(c p) t -> p c t", p=128)), dma=True, sem="ldx")
    op_ldx2 = [op_ldx, op_ldx]

    esPs1 = ExitStack()
    ps_sum = esPs1.enter_context(nc.psum_tensor("ps_sum", [1, T], F32))
    ps_sq = esPs1.enter_context(nc.psum_tensor("ps_sq", [1, T], F32))

    sq_ops = []
    last_sq_w = {}
    for tf in range(NTF):
        sl = slice(tf * TF, (tf + 1) * TF)
        for kc in range(KC):
            par = kc % 2
            op_sq = s.op("vector", lambda e, kc=kc, sl=sl, par=par: e.tensor_tensor(
                sq_sc[:, par, :], xbf[:, kc, sl], xbf[:, kc, sl], ALU.mult),
                deps=[op_ldx2[kc // 4], op_ones, last_sq_w.get(par)])
            op_m1 = s.op("tensor", lambda e, kc=kc, sl=sl: e.matmul(
                ps_sum[0:1, sl], onescol[:, :], xbf[:, kc, sl],
                start=(kc == 0), stop=(kc == KC - 1)), deps=[op_ldx2[kc // 4], op_ones])
            op_m2 = s.op("tensor", lambda e, par=par, sl=sl, kc=kc: e.matmul(
                ps_sq[0:1, sl], onescol[:, :], sq_sc[:, par, :],
                start=(kc == 0), stop=(kc == KC - 1)), deps=[op_sq])
            last_sq_w[par] = op_m2
            if kc == KC - 1:
                sq_ops.append((op_m1, op_m2))
    stat_deps = [o for pair in sq_ops for o in pair]
    r = lambda i: rows[0:1, i, :]
    op_mean = s.op("scalar", lambda e: e.activation(r(0), ps_sum[0:1, :], AF.Copy,
                   bias=0.0, scale=1.0 / D), deps=stat_deps)
    op_msq = s.op("scalar", lambda e: e.activation(r(2), ps_sq[0:1, :], AF.Copy,
                  bias=0.0, scale=1.0 / D), deps=stat_deps)
    op_m2r = s.op("vector", lambda e: e.tensor_tensor(r(1), r(0), r(0), ALU.mult),
                  deps=[op_mean])
    op_var = s.op("vector", lambda e: e.tensor_tensor(r(1), r(2), r(1), ALU.subtract),
                  deps=[op_msq, op_m2r])
    op_lnv = s.op("scalar", lambda e: e.activation(r(2), r(1), AF.Ln,
                  bias=miscrow[0:1, 0:1], scale=1.0), deps=[op_var] + ld_consts[-1:])
    op_a = s.op("scalar", lambda e: e.activation(r(3), r(2), AF.Exp, bias=0.0,
                scale=-0.5), deps=[op_lnv])
    op_b = s.op("vector", lambda e: e.tensor_tensor(r(0), r(0), r(3), ALU.mult),
                deps=[op_a])
    op_e1 = s.op("scalar", lambda e: e.activation(r(2), r(2), AF.Exp, bias=0.0,
                 scale=-1.0), deps=[op_lnv, op_a])
    op_ratio = s.op("vector", lambda e: e.tensor_tensor(r(1), r(1), r(2), ALU.mult),
                    deps=[op_e1, op_var])
    op_lnr = s.op("scalar", lambda e: e.activation(r(1), r(1), AF.Ln,
                  bias=miscrow[0:1, 1:2], scale=1.0), deps=[op_ratio])
    op_rms = s.op("scalar", lambda e: e.activation(r(1), r(1), AF.Exp, bias=0.0,
                  scale=-0.5), deps=[op_lnr])
    esPs1.close()

    es_bc = ExitStack()
    ps_bc = [es_bc.enter_context(nc.psum_tensor(f"ps_bc{i}", [128, TF], F32))
             for i in range(2)]
    op_xn_done = []
    last_t1 = {}
    last_bct = None
    for tf in range(NTF):
        sl = slice(tf * TF, (tf + 1) * TF)
        bc_ops = []
        for i, (row_i, dep) in enumerate(((3, op_a), (0, op_b), (1, op_rms))):
            par = i % 2
            op_bm = s.op("tensor", lambda e, row_i=row_i, sl=sl, par=par: e.matmul(
                ps_bc[par][:, :], onesrow[0:1, :], rows[0:1, row_i, sl],
                start=True, stop=True), deps=[dep, bc_ops[-2] if i >= 2 else None,
                                              last_bct])
            op_bc = s.op("vector", lambda e, i=i, sl=sl, par=par: e.tensor_copy(
                bct[:, i, :], ps_bc[par][:, :]), deps=[op_bm])
            bc_ops.append(op_bc)
        for kc in range(KC):
            par = kc % 2
            op_t1 = s.op("vector", lambda e, kc=kc, sl=sl, par=par: e.tensor_tensor(
                t1_sc[:, par, :], xbf[:, kc, sl], bct[:, 0, :], ALU.mult),
                deps=bc_ops + [last_t1.get(par)])
            op_xn = s.op("vector", lambda e, kc=kc, sl=sl, par=par: e.tensor_tensor(
                xn[:, kc, sl], t1_sc[:, par, :], bct[:, 1, :], ALU.subtract),
                deps=[op_t1])
            last_t1[par] = op_xn
            op_h = s.op("vector", lambda e, kc=kc, sl=sl: e.scalar_tensor_tensor(
                hsb[:, kc, sl], bct[:, 2, :], n1pp[:, kc:kc + 1], xn[:, kc, sl],
                ALU.mult, ALU.mult), deps=[op_xn])
            op_xn_done.append(op_h)
            last_bct = op_h
    dbg_store("d_xn", xn[:, :, :], op_xn_done, "(c p) t -> p c t")
    dbg_store("d_h", hsb[:, :, :], op_xn_done, "(c p) t -> p c t")
    es_bc.close()
    s.barrier()
    esP1.close()
    if upto <= 1:
        return _finish(nc, s, store_ops, outT, xn, dbg_out, esL, esR0,
                       (esRc, esR2, esR1))

    qkv_done = []
    esQ = ExitStack()
    wq_sb = sbR(esQ, "wq_sb", (128, KC, 256), BF16)
    wk_sb = sbR(esQ, "wk_sb", (128, KC, 256), BF16)
    wv_sb = sbR(esQ, "wv_sb", (128, KC, 256), BF16)
    qrot = sbR(esQ, "qrot", (128, 2, T), BF16)
    krot = sbR(esQ, "krot", (128, 2, T), BF16)
    vsb = sbR(esQ, "vsb", (128, NTT, 260), BF16)
    qtmp = sbR(esQ, "qtmp", (128, T), BF16)
    rtmp = sbR(esQ, "rtmp", (128, 4, TF), F32)
    pt_sc = sbR(esQ, "pt_sc", (128, 8, TF), BF16)
    chpp = sbR(esQ, "chpp", (128, 4), F32)
    qsq = sbR(esQ, "qsq", (128, T), BF16)
    lb_sb = sbR(esQ, "lb_sb", (64, TF), F32)
    lrow = sbR(esQ, "lrow", (1, TF), F32)
    kmx = sbR(esQ, "kmx", (1, 16), F32)
    for quarter in range(4):
        ecols = slice(quarter * 256, (quarter + 1) * 256)
        ld_q = s.op("gpsimd", lambda e, ecols=ecols, t=wq_sb: e.dma_start(
            t[:, :, :], wqT.rearrange("(c p) e -> p c e", p=128)[:, :, ecols]),
            dma=True, sem="lwq")
        ld_k = s.op("gpsimd", lambda e, ecols=ecols, t=wk_sb: e.dma_start(
            t[:, :, :], wkT.rearrange("(c p) e -> p c e", p=128)[:, :, ecols]),
            dma=True, sem="lwk")
        ld_v = s.op("gpsimd", lambda e, ecols=ecols, t=wv_sb: e.dma_start(
            t[:, :, :], wvT.rearrange("(c p) e -> p c e", p=128)[:, :, ecols]),
            dma=True, sem="lwv")

        esPq1 = ExitStack()
        ps_mm = [esPq1.enter_context(nc.psum_tensor(f"ps_mm{quarter}_{i}", [128, TF], F32))
                 for i in range(2)]
        ps_rot = [esPq1.enter_context(nc.psum_tensor(f"ps_rot{quarter}_{i}", [128, TF], F32))
                  for i in range(2)]
        last_mm = {}
        last_rot = {}
        last_qtmp = {}
        rope_done = []
        pend_r = None
        for which, wsb, ldw, rot_dst in (("q", wq_sb, ld_q, qrot), ("k", wk_sb, ld_k, krot)):
            for ml in range(2):
                for tf in range(NTF):
                    sl = slice(tf * TF, (tf + 1) * TF)
                    par = tf % 2
                    def mmfn(e, wsb=wsb, ml=ml, sl=sl, par=par):
                        for kc in range(KC):
                            i = e.matmul(ps_mm[par][:, :],
                                         wsb[:, kc, ml * 128:(ml + 1) * 128],
                                         hsb[:, kc, sl],
                                         start=(kc == 0), stop=(kc == KC - 1))
                        return i
                    op_mm = s.op("tensor", mmfn,
                                 deps=[ldw, last_mm.get(par)] + op_xn_done[-1:])
                    op_ev = s.op("vector", lambda e, sl=sl, par=par: e.tensor_copy(
                        qtmp[:, sl], ps_mm[par][:, :]),
                        deps=[op_mm, last_qtmp.get(tf)])
                    last_mm[par] = op_ev
                    if pend_r is not None:
                        pend_r()
                        pend_r = None

                    def mk_rot(sl=sl, par=par, tf=tf, ml=ml, rot_dst=rot_dst,
                               op_ev=op_ev):
                        op_pr = s.op("tensor", lambda e: e.matmul(
                            ps_rot[par][:, :], mrot[:, :], qtmp[:, sl],
                            start=True, stop=True), deps=[op_ev, last_rot.get(par)])
                        last_qtmp[tf] = op_pr
                        op_sin = s.op("vector", lambda e: e.tensor_tensor(
                            rtmp[:, par, :], ps_rot[par][:, :], sin_sb[:, sl], ALU.mult),
                            deps=[op_pr])
                        last_rot[par] = op_sin
                        op_cos = s.op("vector", lambda e: e.tensor_tensor(
                            rtmp[:, 2 + par, :], qtmp[:, sl], cos_sb[:, sl], ALU.mult),
                            deps=[op_ev])
                        op_fin = s.op("vector", lambda e: e.tensor_tensor(
                            rot_dst[:, ml, sl], rtmp[:, 2 + par, :],
                            rtmp[:, par, :], ALU.add), deps=[op_sin, op_cos])
                        rope_done.append(op_fin)
                    pend_r = mk_rot
        pend_r()
        pend_r = None
        esPq1.close()
        esPq = ExitStack()
        ps_v = [esPq.enter_context(nc.psum_tensor(f"ps_v{quarter}_{i}", [128, 256], F32))
                for i in range(2)]
        ps_n = esPq.enter_context(nc.psum_tensor(f"ps_nn{quarter}", [1, 4, TF], F32))
        v_done = []
        last_v = {}
        op_vones = s.op("vector", lambda e: e.memset(
            vsb.ap().rearrange("p c (h x) -> p c h x", x=65)[:, :, :, 64:65], 1.0))
        ch_done = [None] * 4
        last_red = None
        last_nq = {}
        prev_lnb = None
        prev_chb = None

        def norm_pair(pairi):
            nonlocal last_red, prev_lnb, prev_chb
            ml = pairi
            for src_rot, outoff in ((krot, 0), (qrot, 1)):
                sq_ops = []
                for hb in range(2):
                    rsl = slice(hb * 64, hb * 64 + 64)
                    op_sq = s.op("vector", lambda e, rsl=rsl, ml=ml, src_rot=src_rot:
                                 e.tensor_tensor(qsq[rsl, :], src_rot[rsl, ml, :],
                                                 src_rot[rsl, ml, :], ALU.mult),
                                 deps=rope_done + [last_red])
                    sq_ops.append(op_sq)
                reds = {0: [], 1: []}
                for tf in range(NTF):
                    for hb in range(2):
                        rsl = slice(hb * 64, hb * 64 + 64)
                        nb = 2 * hb + tf % 2
                        op_nm = s.op("tensor", lambda e, rsl=rsl, tf=tf, nb=nb:
                                     e.matmul(ps_n[0:1, nb, :], onescol[rsl, 0:1],
                                              qsq[rsl, tf * TF:(tf + 1) * TF],
                                              start=True, stop=True),
                                     deps=[sq_ops[hb], last_nq.get(nb)])
                        op_rd = s.op("vector", lambda e, tf=tf, hb=hb, nb=nb:
                                     e.tensor_reduce(
                            kmx[0:1, 2 + 6 * hb + tf:3 + 6 * hb + tf],
                            ps_n[0:1, nb, :], AX.X, ALU.max),
                            deps=[op_nm, prev_lnb])
                        last_nq[nb] = op_rd
                        reds[hb].append(op_rd)
                for hb in range(2):
                    col = outoff + 8 * hb
                    op_mx = s.op("vector", lambda e, col=col, hb=hb: e.tensor_reduce(
                        kmx[0:1, col:col + 1], kmx[0:1, 2 + 6 * hb:6 + 6 * hb],
                        AX.X, ALU.max), deps=reds[hb])
                    last_red = op_mx
            for hb in range(2):
                hl = 2 * pairi + hb
                c0 = 8 * hb
                op_lnb = s.op("scalar", lambda e, c0=c0: e.activation(
                    srow[0:1, 0:2], kmx[0:1, c0:c0 + 2], AF.Ln, bias=0.0, scale=1.0),
                    deps=[last_red, prev_chb])
                prev_lnb = op_lnb
                op_ch1 = s.op("vector", lambda e: e.tensor_tensor(
                    srow[0:1, 2:3], srow[0:1, 0:1], srow[0:1, 1:2], ALU.add),
                    deps=[op_lnb, prev_chb])
                op_ch = s.op("vector", lambda e: e.tensor_scalar(
                    srow[0:1, 2:3], srow[0:1, 2:3], -0.5, float(np.log(8.0)),
                    ALU.mult, ALU.add), deps=[op_ch1])
                op_chb = s.op("tensor", lambda e: e.matmul(
                    ps_v[0][:, 0:1], onesrow[0:1, :], srow[0:1, 2:3],
                    start=True, stop=True), deps=[op_ch, last_v.get(0)])
                op_chc = s.op("vector", lambda e, hl=hl: e.tensor_copy(
                    chpp[:, hl:hl + 1], ps_v[0][:, 0:1]), deps=[op_chb])
                last_v[0] = op_chc
                ch_done[hl] = (op_chc, op_lnb, op_chb)
                prev_chb = op_chb

        for tt in range(NTT):
            par = tt % 2
            def vfn(e, tt=tt, par=par):
                for kc in range(KC):
                    i = e.matmul(ps_v[par][:, :], hsb[:, kc, tt * 128:(tt + 1) * 128],
                                 wv_sb[:, kc, :], start=(kc == 0), stop=(kc == KC - 1))
                return i
            op_vm = s.op("tensor", vfn,
                         deps=[ld_v, last_v.get(par), rope_done[-1]])
            def vefn(e, tt=tt, par=par):
                for hb in range(4):
                    i = e.tensor_copy(vsb[:, tt, hb * 65:hb * 65 + 64],
                                      ps_v[par][:, hb * 64:(hb + 1) * 64])
                return i
            op_ve = s.op("vector", vefn, deps=[op_vm, op_vones])
            last_v[par] = op_ve
            v_done.append(op_ve)
            if tt == 7:
                norm_pair(0)
            elif tt == 15:
                norm_pair(1)
        if quarter == 0:
            dbg_store("d_v", vsb[:, :, :], v_done, "(c p) e -> p c e")
            dbg_store("d_q", qrot[:, :, :], rope_done, "(c p) t -> p c t")
            dbg_store("d_k", krot[:, :, :], rope_done, "(c p) t -> p c t")
        esPq.close()

        norms_all = ch_done[3][0]
        esPa = ExitStack()
        ps_st = [esPa.enter_context(nc.psum_tensor(f"ps_st{quarter}_{i}", [128, TF], F32))
                 for i in range(4)]
        ps_cl = [esPa.enter_context(nc.psum_tensor(f"ps_cl{quarter}_{i}", [65, TF], F32))
                 for i in range(2)]
        ps_lb = esPa.enter_context(nc.psum_tensor(f"ps_lb{quarter}", [64, TF], F32))

        last_stpar = {i: norms_all for i in range(4)}
        last_pt = {}
        evac_done = [norms_all, norms_all]
        last_lb = [norms_all, norms_all]
        evac_pend = None
        for pairi in range(2):
            ml = pairi
            heads = (2 * pairi, 2 * pairi + 1)
            chcs = {hb: ch_done[2 * pairi + hb][0] for hb in range(2)}
            for qc in range(NTF):
                qbase = qc * TF
                nkt = 4 * qc + 4
                clp = qc % 2
                pend = []
                for kt in range(nkt):
                    j = kt - 4 * qc
                    off = max(0, j) * 128
                    ssl = slice(off, TF)
                    qsl = slice(qbase + off, qbase + TF)
                    prs = []
                    for hb in range(2):
                        hl = 2 * pairi + hb
                        rsl = slice(hb * 64, hb * 64 + 64)
                        spar = 2 * hb + (kt % 2)
                        ptx = 4 * hb + (kt % 4)
                        op_st = s.op("tensor", lambda e, rsl=rsl, ml=ml, kt=kt,
                                     spar=spar, ssl=ssl, qsl=qsl: e.matmul(
                            ps_st[spar][:, ssl], krot[rsl, ml, kt * 128:(kt + 1) * 128],
                            qrot[rsl, ml, qsl], start=True, stop=True),
                            deps=[last_stpar.get(spar), chcs[hb]])
                        if kt == 2 and hb == 0 and evac_pend is not None:
                            evac_pend()
                            evac_pend = None
                        op_exp = s.op("scalar", lambda e, spar=spar, ptx=ptx,
                                      ssl=ssl, hl=hl: e.activation(
                            pt_sc[:, ptx, ssl], ps_st[spar][:, ssl], AF.Exp,
                            bias=chpp[:, hl:hl + 1], scale=0.125),
                            deps=[op_st, last_pt.get(ptx)])
                        last_stpar[spar] = op_exp
                        pt_ready = op_exp
                        if j >= 0:
                            pt_ready = s.op("vector", lambda e, ptx=ptx, off=off:
                                            e.tensor_tensor(
                                pt_sc[:, ptx, off:off + 128],
                                pt_sc[:, ptx, off:off + 128],
                                tri[:, :], ALU.mult), deps=[op_exp])
                        prs.append(pt_ready)

                    def mk_lav(prs=prs, ssl=ssl, kt=kt, nkt=nkt, pairi=pairi,
                               clp=clp):
                        o_avs = []
                        for hb in range(2):
                            hl = 2 * pairi + hb
                            ptx = 4 * hb + (kt % 4)
                            o_av = s.op("tensor", lambda e, ptx=ptx, ssl=ssl,
                                        kt=kt, nkt=nkt, hl=hl, hb=hb, clp=clp:
                                        e.matmul(
                                ps_cl[hb][:, ssl], vsb[:, kt, hl * 65:(hl + 1) * 65],
                                pt_sc[:, ptx, ssl], start=(kt == 0),
                                stop=(kt == nkt - 1)),
                                deps=[prs[hb]] + ([evac_done[hb]] if kt == 0 else [])
                                     + v_done)
                            last_pt[ptx] = o_av
                            o_avs.append(o_av)
                        mk_lav.res = o_avs
                    if len(pend) >= 2:
                        pend.pop(0)()
                    pend.append(mk_lav)
                if evac_pend is not None:
                    evac_pend()
                    evac_pend = None
                for fn in pend:
                    fn()
                av_lasts = fn.res

                def mk_evac(av_lasts=av_lasts, pairi=pairi, qbase=qbase, hh0=quarter * 4 + 2 * pairi):
                    for hb in range(2):
                        hh = hh0 + hb
                        r0 = hb * 64
                        op_ln_l = s.op("scalar", lambda e, hb=hb: e.activation(
                            lrow[0:1, :], ps_cl[hb][64:65, :], AF.Ln, bias=0.0,
                            scale=1.0), deps=[av_lasts[hb], last_lb[0]])
                        op_linv = s.op("scalar", lambda e: e.activation(
                            lrow[0:1, :], lrow[0:1, :], AF.Exp, bias=0.0, scale=-1.0),
                            deps=[op_ln_l])
                        op_lb = s.op("tensor", lambda e: e.matmul(
                            ps_lb[:, :], onesrow[0:1, 0:64], lrow[0:1, :],
                            start=True, stop=True), deps=[op_linv, last_lb[1]])
                        op_lbc = s.op("vector", lambda e: e.tensor_copy(
                            lb_sb[:, :], ps_lb[:, :]),
                            deps=[op_lb, evac_done[1 - hb], evac_done[hb]])
                        op_cev = s.op("vector", lambda e, r0=r0, hh=hh, qbase=qbase,
                                      hb=hb: e.tensor_tensor(
                            ctxT[r0:r0 + 64, hh // 2, qbase:qbase + TF],
                            ps_cl[hb][0:64, :], lb_sb[:, :], ALU.mult),
                            deps=[op_lbc, av_lasts[hb]])
                        evac_done[hb] = op_cev
                        last_lb[0] = op_lb
                        last_lb[1] = op_lbc
                        qkv_done.append(op_cev)
                evac_pend = mk_evac
        if evac_pend is not None:
            evac_pend()
            evac_pend = None
        esPa.close()
        s.barrier()
    esQ.close()
    dbg_store("d_ctx", ctxT[:, :, :], [s.last["vector"]], "(c p) t -> p c t")
    if upto <= 2:
        return _finish(nc, s, store_ops, outT, xn, dbg_out, esL, esR0,
                       (esRc, esR2, esR1))

    esRc.close()
    racc = sbL("racc", (128, KC, T), F32)
    esP3 = ExitStack()
    wo_sb = sbR(esP3, "wo_sb", (128, KC, D), BF16)
    ld_wo = s.op("gpsimd", lambda e: e.dma_start(
        wo_sb[:, :, :], woT.rearrange("(c p) e -> p c e", p=128)), dma=True, sem="lwo")
    esPo = ExitStack()
    ps_o = [esPo.enter_context(nc.psum_tensor(f"ps_o{i}", [128, TF], F32))
            for i in range(2)]
    last_o = {}
    racc_done = []
    for tf in range(NTF):
        sl = slice(tf * TF, (tf + 1) * TF)
        for mc in range(KC):
            par = (mc * NTF + tf) % 2
            def ofn(e, mc=mc, sl=sl, par=par):
                for kc in range(KC):
                    i = e.matmul(ps_o[par][:, :], wo_sb[:, kc, mc * 128:(mc + 1) * 128],
                                 ctxT[:, kc, sl], start=(kc == 0), stop=(kc == KC - 1))
                return i
            op_om = s.op("tensor", ofn, deps=[ld_wo, last_o.get(par)])
            op_oe = s.op("vector", lambda e, mc=mc, sl=sl, par=par: e.tensor_tensor(
                racc[:, mc, sl], ps_o[par][:, :], xn[:, mc, sl], ALU.add),
                deps=[op_om])
            last_o[par] = op_oe
            racc_done.append(op_oe)

    dbg_store("d_racc", racc[:, :, :], racc_done, "(c p) t -> p c t")
    esPo.close()
    s.barrier()
    esP3.close()
    esR2.close()
    if upto <= 3:
        return _finish(nc, s, store_ops, outT, racc, dbg_out, esL, esR0,
                       (esR2, esR1))

    op_abs = s.op("scalar", lambda e: e.activation(srow[0:1, 2:4], scl[0:1, 0:2],
                  AF.Abs, bias=0.0, scale=1.0), deps=ld_consts)
    op_lns = s.op("scalar", lambda e: e.activation(srow[0:1, 2:4], srow[0:1, 2:4],
                  AF.Ln, bias=0.0, scale=1.0), deps=[op_abs])
    op_exs = s.op("scalar", lambda e: e.activation(srow[0:1, 2:4], srow[0:1, 2:4],
                  AF.Exp, bias=0.0, scale=-1.0), deps=[op_lns])
    op_sgn = s.op("scalar", lambda e: e.activation(srow[0:1, 6:8], scl[0:1, 0:2],
                  AF.Sign, bias=0.0, scale=1.0), deps=[op_exs])
    op_rr = s.op("vector", lambda e: e.tensor_tensor(
        srow[0:1, 2:4], srow[0:1, 2:4], srow[0:1, 6:8], ALU.mult), deps=[op_sgn])
    op_sp = s.op("vector", lambda e: e.tensor_tensor(
        srow[0:1, 4:5], scl[0:1, 0:1], scl[0:1, 1:2], ALU.mult), deps=[op_rr])
    esPsc = ExitStack()
    ps_sc = esPsc.enter_context(nc.psum_tensor("ps_sc", [128, 4], F32))
    op_spm = s.op("tensor", lambda e: e.matmul(
        ps_sc[:, 0:3], onesrow[0:1, :], srow[0:1, 2:5],
        start=True, stop=True), deps=[op_sp])
    op_spc = s.op("vector", lambda e: e.tensor_copy(spp[:, 0:3], ps_sc[:, 0:3]),
                  deps=[op_spm])
    esPsc.close()
    xnq_done = []
    for mc in range(KC):
        op_q8 = s.op("scalar", lambda e, mc=mc: e.activation(
            xnq[:, mc, :], xn[:, mc, :], AF.Copy, bias=0.0, scale=spp[:, 0:1]),
            deps=[op_spc])
        xnq_done.append(op_q8)
    esR1.close()

    esP5 = ExitStack()
    NSLOT = 4
    slots = [sbR(esP5, f"wslot{i}", (128, KC, 1024), BF16) for i in range(NSLOT)]
    mats = [(sgT, None), (suT, None), (sdT, None)]
    for e_i in range(E):
        mats += [(egT, e_i), (euT, e_i), (edT, e_i)]
    dma_ev = [None] * len(mats)
    cons_ev = [None] * len(mats)

    def append_dma(idx):
        src_, e_i = mats[idx]
        slot = slots[idx % NSLOT]
        if e_i is None:
            apsrc = src_.rearrange("(c p) n -> p c n", p=128)
        else:
            apsrc = src_[e_i].rearrange("(c p) n -> p c n", p=128)
        dma_ev[idx] = s.op("gpsimd", lambda e, t=slot, a=apsrc: e.dma_start(
            t[:, :, :], a),
            deps=[cons_ev[idx - NSLOT]] if idx >= NSLOT else [xnq_done[-1]],
            dma=True, sem=f"ls{idx % 4}")

    for _i in range(NSLOT):
        dma_ev[_i] = None
    def _start_dmas():
        for _i in range(NSLOT):
            append_dma(_i)
    _sd_done = False
    _start_dmas()
    h2 = sbL("h2", (128, KC, T), BF16)
    econ = sbL("econ", (16, 16 * 128), BF16)
    cwT = sbL("cwT", (16, T), BF16)
    ebias_bc = sbL("ebias_bc", (128, E), F32)
    ld_econ = s.op("sync", lambda e: e.dma_start(econ[:, :], econ_h), dma=True,
                   sem="lde")
    esP4 = ExitStack()
    rows2 = sbR(esP4, "rows2", (1, 1, T), F32)
    rsq_sc4 = sbR(esP4, "rsq_sc4", (128, 2, TF), BF16)
    r2bc = sbR(esP4, "r2bc", (128, T), F32)
    gsc = sbR(esP4, "gsc", (128, 4, E), F32)
    cwbf = sbR(esP4, "cwbf", (128, E), BF16)
    tcol = sbR(esP4, "tcol", (128, 4), F32)
    esPs4 = ExitStack()
    ps_sq2 = esPs4.enter_context(nc.psum_tensor("ps_sq2", [1, T], F32))
    last_rsq = {}
    sq2_ops = []
    stat_pend = []
    for tf in range(NTF):
        sl = slice(tf * TF, (tf + 1) * TF)
        for mc in range(KC):
            def mk_stat(mc=mc, sl=sl):
                spar = mc % 2
                op_r2 = s.op("vector", lambda e, mc=mc, sl=sl, spar=spar:
                             e.tensor_tensor(
                    rsq_sc4[:, spar, :], racc[:, mc, sl], racc[:, mc, sl], ALU.mult),
                    deps=[last_rsq.get(spar)])
                op_sm = s.op("tensor", lambda e, sl=sl, spar=spar, mc=mc: e.matmul(
                    ps_sq2[0:1, sl], onescol[:, 0:1], rsq_sc4[:, spar, :],
                    start=(mc == 0), stop=(mc == KC - 1)), deps=[op_r2])
                last_rsq[spar] = op_sm
                if mc == KC - 1:
                    sq2_ops.append(op_sm)
            if len(stat_pend) >= 2:
                stat_pend.pop(0)()
            stat_pend.append(mk_stat)
    for fn in stat_pend:
        fn()
    op_ms2 = s.op("scalar", lambda e: e.activation(rows2[0:1, 0, :], ps_sq2[0:1, :],
                  AF.Copy, bias=0.0, scale=1.0 / D), deps=sq2_ops)
    op_ln2 = s.op("scalar", lambda e: e.activation(rows2[0:1, 0, :], rows2[0:1, 0, :],
                  AF.Ln, bias=miscrow[0:1, 1:2], scale=1.0), deps=[op_ms2])
    op_rm2 = s.op("scalar", lambda e: e.activation(rows2[0:1, 0, :], rows2[0:1, 0, :],
                  AF.Exp, bias=0.0, scale=-0.5), deps=[op_ln2])
    esPs4.close()
    es_bc2 = ExitStack()
    ps_bc2 = [es_bc2.enter_context(nc.psum_tensor(f"ps_bc2{i}", [128, TF], F32))
              for i in range(2)]
    bc2_done = []
    last_bc2 = {}
    for tf in range(NTF):
        sl = slice(tf * TF, (tf + 1) * TF)
        par = tf % 2
        op_bm = s.op("tensor", lambda e, sl=sl, par=par: e.matmul(
            ps_bc2[par][:, :], onesrow[0:1, :], rows2[0:1, 0, sl], start=True, stop=True),
            deps=[op_rm2, last_bc2.get(par)])
        op_bc = s.op("vector", lambda e, sl=sl, par=par: e.tensor_copy(
            r2bc[:, sl], ps_bc2[par][:, :]), deps=[op_bm])
        last_bc2[par] = op_bc
        bc2_done.append(op_bc)
    h2_done = []
    for tf in range(NTF):
        sl = slice(tf * TF, (tf + 1) * TF)
        for mc in range(KC):
            op_h2 = s.op("vector", lambda e, mc=mc, sl=sl: e.scalar_tensor_tensor(
                h2[:, mc, sl], r2bc[:, sl], n2pp[:, mc:mc + 1], racc[:, mc, sl],
                ALU.mult, ALU.mult), deps=bc2_done)
            h2_done.append(op_h2)
    dbg_store("d_h2", h2[:, :, :], h2_done, "(c p) t -> p c t")
    es_bc2.close()

    rw_sb = sbR(esP4, "rw_sb", (128, KC, E), BF16)
    ld_rw = s.op("gpsimd", lambda e: e.dma_start(
        rw_sb[:, :, :], rwT.rearrange("(c p) e -> p c e", p=128)), dma=True, sem="lwr")
    esPr = ExitStack()
    ps_lg = [esPr.enter_context(nc.psum_tensor(f"ps_lg{i}", [128, E], F32))
             for i in range(2)]
    ps_tr = esPr.enter_context(nc.psum_tensor("ps_tr", [16, 128], BF16))
    op_ebm = s.op("tensor", lambda e: e.matmul(
        ps_lg[0][:, :], onesrow[0:1, :], ebias[0:1, :], start=True, stop=True),
        deps=bc2_done)
    op_ebc = s.op("vector", lambda e: e.tensor_copy(ebias_bc[:, :], ps_lg[0][:, :]),
                  deps=[op_ebm])
    last_lg = {0: op_ebc}
    last_tr = None
    prev_gg = prev_de = None
    cw_done = []
    for tt in range(NTT):
        par = tt % 2
        def lgfn(e, tt=tt, par=par):
            for kc in range(KC):
                i = e.matmul(ps_lg[par][:, :], h2[:, kc, tt * 128:(tt + 1) * 128],
                             rw_sb[:, kc, :], start=(kc == 0), stop=(kc == KC - 1))
            return i
        op_lg = s.op("tensor", lgfn,
                     deps=[ld_rw, last_lg.get(par)]
                          + h2_done[8 * (tt // 4):8 * (tt // 4) + 8])
        op_ga = s.op("vector", lambda e, par=par: e.tensor_tensor(
            gsc[:, 0, :], ps_lg[par][:, :], ebias_bc[:, :], ALU.add),
            deps=[op_lg, op_ebc, prev_gg, prev_de])
        last_lg[par] = op_ga
        op_gx = s.op("scalar", lambda e: e.activation(gsc[:, 1, :], gsc[:, 0, :],
                     AF.Exp, bias=0.0, scale=-1.0), deps=[op_ga])
        op_gl = s.op("scalar", lambda e: e.activation(gsc[:, 1, :], gsc[:, 1, :],
                     AF.Ln, bias=onecolf[:, 0:1], scale=1.0), deps=[op_gx, op_onesf])
        op_gg = s.op("scalar", lambda e: e.activation(gsc[:, 0, :], gsc[:, 1, :],
                     AF.Exp, bias=0.0, scale=-1.0), deps=[op_gl])
        op_m1 = s.op("vector", lambda e: e.tensor_reduce(
            tcol[:, 0:1], gsc[:, 0, :], AX.X, ALU.max), deps=[op_gg])
        op_i1 = s.op("vector", lambda e: e.tensor_scalar(
            gsc[:, 1, :], gsc[:, 0, :], tcol[:, 0:1], None, ALU.is_ge), deps=[op_m1])
        op_g2 = s.op("vector", lambda e: e.scalar_tensor_tensor(
            gsc[:, 2, :], gsc[:, 1, :], -1e9, gsc[:, 0, :], ALU.mult, ALU.add),
            deps=[op_i1])
        op_m2 = s.op("vector", lambda e: e.tensor_reduce(
            tcol[:, 1:2], gsc[:, 2, :], AX.X, ALU.max), deps=[op_g2])
        op_i2 = s.op("vector", lambda e: e.tensor_scalar(
            gsc[:, 2, :], gsc[:, 2, :], tcol[:, 1:2], None, ALU.is_ge), deps=[op_m2])
        op_dn = s.op("vector", lambda e: e.tensor_tensor(
            tcol[:, 2:3], tcol[:, 0:1], tcol[:, 1:2], ALU.add), deps=[op_m2])
        op_dn2 = s.op("vector", lambda e: e.tensor_scalar(
            tcol[:, 2:3], tcol[:, 2:3], 1e-9, None, ALU.add), deps=[op_dn])
        op_dl = s.op("scalar", lambda e: e.activation(tcol[:, 2:3], tcol[:, 2:3],
                     AF.Ln, bias=0.0, scale=1.0), deps=[op_dn2])
        op_de = s.op("scalar", lambda e: e.activation(tcol[:, 2:3], tcol[:, 2:3],
                     AF.Exp, bias=0.0, scale=-1.0), deps=[op_dl])
        op_w1 = s.op("vector", lambda e: e.tensor_tensor(
            tcol[:, 0:1], tcol[:, 0:1], tcol[:, 2:3], ALU.mult), deps=[op_de])
        op_w2 = s.op("vector", lambda e: e.tensor_tensor(
            tcol[:, 1:2], tcol[:, 1:2], tcol[:, 2:3], ALU.mult), deps=[op_de])
        op_cw1 = s.op("vector", lambda e: e.tensor_scalar(
            gsc[:, 3, :], gsc[:, 1, :], tcol[:, 0:1], None, ALU.mult), deps=[op_w1, op_i2])
        op_cw = s.op("vector", lambda e: e.scalar_tensor_tensor(
            cwbf[:, :], gsc[:, 2, :], tcol[:, 1:2], gsc[:, 3, :], ALU.mult, ALU.add),
            deps=[op_cw1, op_w2])
        op_tr = s.op("tensor", lambda e: e.transpose(
            ps_tr[:, :], cwbf[:, :], ident[:, :]), deps=[op_cw, last_tr])
        op_tc = s.op("vector", lambda e, tt=tt: e.tensor_copy(
            cwT[0:16, tt * 128:(tt + 1) * 128], ps_tr[:, :]), deps=[op_tr])
        last_tr = op_tc
        prev_gg, prev_de = op_gg, op_de
        cw_done.append(op_tc)
    dbg_store("d_cw", cwT[:, :], cw_done)
    esPr.close()

    s.barrier()
    esP4.close()
    if upto <= 4:
        return _finish(nc, s, store_ops, outT, racc, dbg_out, esL, esR0,
                       (esR1,))

    z_sb = sbR(esP5, "z_sb", (128, KC, TF), BF16)
    wbc = sbR(esP5, "wbc", (128, T), BF16)
    zsil = sbR(esP5, "zsil", (128, 2, TF), BF16)
    tz = sbR(esP5, "tz", (128, 2, TF), F32)

    esPm = ExitStack()
    ps_g = [esPm.enter_context(nc.psum_tensor(f"ps_g{i}", [128, TF], F32)) for i in range(2)]
    ps_u = [esPm.enter_context(nc.psum_tensor(f"ps_u{i}", [128, TF], F32)) for i in range(2)]
    ps_y = [esPm.enter_context(nc.psum_tensor(f"ps_y{i}", [128, TF], F32)) for i in range(2)]
    ps_wb = esPm.enter_context(nc.psum_tensor("ps_wb", [128, TF], F32))

    last_gp = {}
    last_up = {}
    last_yp = {}
    last_wbc = None
    last_zsil = {}
    last_tz = {}
    moe_done = []
    for set_i in range(17):
        e_i = set_i - 1
        ig, iu, idn = 3 * set_i, 3 * set_i + 1, 3 * set_i + 2
        sg_slot, su_slot, sd_slot = (slots[ig % NSLOT], slots[iu % NSLOT],
                                     slots[idn % NSLOT])
        wbc_ops = []
        if e_i >= 0:
            for tf in range(NTF):
                sl = slice(tf * TF, (tf + 1) * TF)
                op_wm = s.op("tensor", lambda e, e_i=e_i, sl=sl: e.matmul(
                    ps_wb[:, :], econ[0:16, e_i * 128:(e_i + 1) * 128], cwT[0:16, sl],
                    start=True, stop=True), deps=cw_done + [last_wbc, ld_econ])
                op_wc = s.op("vector", lambda e, sl=sl: e.tensor_copy(
                    wbc[:, sl], ps_wb[:, :]), deps=[op_wm])
                last_wbc = op_wc
                wbc_ops.append(op_wc)
        g_cons = u_cons = d_cons = None
        for tf in range(NTF):
            sl = slice(tf * TF, (tf + 1) * TF)
            z_ready = []
            for mc in range(KC):
                par = mc % 2
                def gfn(e, mc=mc, sl=sl, par=par, w=sg_slot):
                    for kc in range(KC):
                        i = e.matmul(ps_g[par][:, :], w[:, kc, mc * 128:(mc + 1) * 128],
                                     h2[:, kc, sl], start=(kc == 0), stop=(kc == KC - 1))
                    return i
                op_gm = s.op("tensor", gfn, deps=[dma_ev[ig], last_gp.get(par)])
                g_cons = op_gm
                op_sl = s.op("scalar", lambda e, par=par: e.activation(
                    zsil[:, par, :], ps_g[par][:, :], AF.Silu, bias=0.0, scale=1.0),
                    deps=[op_gm, last_zsil.get(par)])
                last_gp[par] = op_sl
                def ufn(e, mc=mc, sl=sl, par=par, w=su_slot):
                    for kc in range(KC):
                        i = e.matmul(ps_u[par][:, :], w[:, kc, mc * 128:(mc + 1) * 128],
                                     h2[:, kc, sl], start=(kc == 0), stop=(kc == KC - 1))
                    return i
                op_um = s.op("tensor", ufn, deps=[dma_ev[iu], last_up.get(par)])
                u_cons = op_um
                if e_i >= 0:
                    op_t = s.op("vector", lambda e, par=par: e.tensor_tensor(
                        tz[:, par, :], ps_u[par][:, :], zsil[:, par, :], ALU.mult),
                        deps=[op_um, op_sl, last_tz.get(par)])
                    last_up[par] = op_t
                    op_z = s.op("vector", lambda e, mc=mc, par=par, sl=sl:
                                e.tensor_tensor(z_sb[:, mc, :], tz[:, par, :],
                                                wbc[:, sl], ALU.mult),
                                deps=[op_t] + wbc_ops)
                    last_tz[par] = op_z
                else:
                    op_z = s.op("vector", lambda e, mc=mc, par=par: e.tensor_tensor(
                        z_sb[:, mc, :], ps_u[par][:, :], zsil[:, par, :], ALU.mult),
                        deps=[op_um, op_sl])
                    last_up[par] = op_z
                last_zsil[par] = op_z
                z_ready.append(op_z)
            for dc in range(KC):
                par = dc % 2
                def yfn1(e, dc=dc, par=par, w=sd_slot):
                    for ic in range(4):
                        i = e.matmul(ps_y[par][:, :], w[:, ic, dc * 128:(dc + 1) * 128],
                                     z_sb[:, ic, :], start=(ic == 0), stop=False)
                    return i
                def yfn2(e, dc=dc, par=par, w=sd_slot):
                    for ic in range(4, KC):
                        i = e.matmul(ps_y[par][:, :], w[:, ic, dc * 128:(dc + 1) * 128],
                                     z_sb[:, ic, :], start=False, stop=(ic == KC - 1))
                    return i
                s.op("tensor", yfn1, deps=z_ready[:4] + [dma_ev[idn],
                     last_yp.get(par)])
                op_ym = s.op("tensor", yfn2, deps=z_ready[4:])
                d_cons = op_ym
                op_ra = s.op("vector", lambda e, dc=dc, sl=sl, par=par: e.tensor_tensor(
                    racc[:, dc, sl], ps_y[par][:, :], racc[:, dc, sl], ALU.add),
                    deps=[op_ym])
                last_yp[par] = op_ra
                moe_done.append(op_ra)
        cons_ev[ig], cons_ev[iu], cons_ev[idn] = g_cons, u_cons, d_cons
        for j in range(3 * set_i + NSLOT, min(3 * set_i + 3 + NSLOT, len(mats))):
            if dma_ev[j] is None:
                append_dma(j)
    dbg_store("d_s", racc[:, :, :], moe_done[-4:], "(c p) t -> p c t")
    esPm.close()
    s.barrier()
    esP5.close()
    if upto <= 5:
        return _finish(nc, s, store_ops, outT, racc, dbg_out, esL, esR0)

    esR6 = ExitStack()
    g_sb = sbR(esR6, "g_sb", (128, KC, T), BF16)
    esP6 = ExitStack()
    kq = sbR(esP6, "kq", (128, KC, 1024), F8)
    fk_sb = sbR(esP6, "fk_sb", (128, KC, 1024), BF16)
    y2_sc = sbR(esP6, "y2_sc", (128, 2, TF), BF16)
    ld_fk = s.op("sync", lambda e: e.dma_start(
        fk_sb[:, :, :], ffnK.rearrange("(c p) f -> p c f", p=128)), dma=True, sem="ldf")
    kq_done = []
    for mc in range(KC):
        op_kq = s.op("scalar", lambda e, mc=mc: e.activation(
            kq[:, mc, :], fk_sb[:, mc, :], AF.Copy, bias=0.0, scale=spp[:, 1:2]),
            deps=[ld_fk, op_spc])
        kq_done.append(op_kq)
    esPf = ExitStack()
    ps_f = [esPf.enter_context(nc.psum_tensor(f"ps_f{i}", [128, TF], F32))
            for i in range(2)]
    last_f = {}
    last_sil = {}
    g_done = []
    for fc in range(KC):
        for tf in range(NTF):
            sl = slice(tf * TF, (tf + 1) * TF)
            par = (fc * NTF + tf) % 2
            def ffn_mm(e, fc=fc, sl=sl, par=par):
                for kc in range(KC):
                    i = e.matmul(ps_f[par][:, :], kq[:, kc, fc * 128:(fc + 1) * 128],
                                 xnq[:, kc, sl], start=(kc == 0), stop=(kc == KC - 1))
                return i
            op_fm = s.op("tensor", ffn_mm, deps=kq_done + xnq_done + [last_f.get(par)])
            op_y2 = s.op("vector", lambda e, fc=fc, par=par: e.tensor_scalar(
                y2_sc[:, par, :], ps_f[par][:, :], spp[:, 2:3], biaspp[:, fc:fc + 1],
                ALU.mult, ALU.add), deps=[op_fm, last_sil.get(par)])
            last_f[par] = op_y2
            op_gs = s.op("scalar", lambda e, fc=fc, sl=sl, par=par: e.activation(
                g_sb[:, fc, sl], y2_sc[:, par, :], AF.Silu, bias=0.0, scale=1.0),
                deps=[op_y2])
            last_sil[par] = op_gs
            g_done.append(op_gs)
    dbg_store("d_g", g_sb[:, :, :], g_done, "(c p) t -> p c t")
    last_ffn_mm = last_f
    esPf.close()
    esP6.close()

    esP7 = ExitStack()
    xf32 = sbR(esP7, "xf32", (128, 2, T), F32)
    tmp7 = sbR(esP7, "tmp7", (128, T), F32)
    osb = sbR(esP7, "osb", (128, 2, T), F32)
    last_x = {}
    last_os = {}
    last_tmp = None
    ffn_pe_done = [last_ffn_mm.get(0), last_ffn_mm.get(1)]
    for mc in range(KC):
        par = mc % 2
        op_lx = s.op("sync", lambda e, mc=mc, par=par: e.dma_start(
            xf32[:, par, :], xT[mc * 128:(mc + 1) * 128, :]),
            deps=[last_x.get(par)] + ffn_pe_done, dma=True, sem=f"lx{par}")
        op_sg = s.op("vector", lambda e, mc=mc: e.tensor_tensor(
            tmp7[:, :], racc[:, mc, :], g_sb[:, mc, :], ALU.mult),
            deps=[last_tmp, g_done[-1], moe_done[-1]])
        op_out = s.op("vector", lambda e, mc=mc, par=par: e.tensor_tensor(
            osb[:, par, :], tmp7[:, :], xf32[:, par, :], ALU.add),
            deps=[op_sg, op_lx, last_os.get(par)])
        last_tmp = op_out
        op_st = s.op("sync", lambda e, mc=mc, par=par: e.dma_start(
            outT[mc * 128:(mc + 1) * 128, :], osb[:, par, :]),
            deps=[op_out], dma=True, sem=f"st{par}")
        last_x[par] = op_out
        last_os[par] = op_st
        store_ops.append(op_st)
    esP7.close()
    esR6.close()
    return _finish(nc, s, store_ops, outT, racc, dbg_out, esL, esR0)


def _finish(nc, s, store_ops, outT, racc, dbg_out, esL, esR0, extra=()):
    if not any(o.sem.startswith(("lds", "st")) for o in store_ops):
        op = s.op("gpsimd", lambda e: e.dma_start(
            outT.rearrange("(c p) t -> p c t", p=128), racc[:, :, :]),
            deps=[s.last.get("vector"), s.last.get("scalar"), s.last.get("tensor")],
            dma=True)
        store_ops.append(op)
    s.wait_only("sync", store_ops)
    s.wait_only("gpsimd", [o for o in store_ops if o.sem == "ldg"])

    with ExitStack() as es:
        sems = {}
        sem_names = set()
        for o in s.ops:
            if o.sem:
                sem_names.add(o.sem)
        for o in s.ops:
            for d in o.deps:
                sem_names.add(d.sem)
        for name in sorted(sem_names):
            sems[name] = es.enter_context(nc.semaphore(name))
        block = es.enter_context(nc.Block())

        @block.tensor
        def _(eng):
            s.emit("tensor", eng, sems)

        @block.vector
        def _(eng):
            s.emit("vector", eng, sems)

        @block.scalar
        def _(eng):
            s.emit("scalar", eng, sems)

        @block.gpsimd
        def _(eng):
            s.emit("gpsimd", eng, sems)

        @block.sync
        def _(eng):
            s.emit("sync", eng, sems)
    for es_x in extra:
        try:
            es_x.close()
        except Exception:
            pass
    esR0.close()
    esL.close()
    return nc


def make_inputs(inputs, consts=None):
    f32 = np.float32
    c = consts or _host_consts()
    x = np.asarray(inputs["L0_attn_x"])
    xTs = [np.ascontiguousarray(x[b].T).astype(f32, copy=False) for b in range(x.shape[0])]
    base = {
        "wqT": np.ascontiguousarray(np.asarray(inputs["L0_attn_wq"]).T),
        "wkT": np.ascontiguousarray(np.asarray(inputs["L0_attn_wk"]).T),
        "wvT": np.ascontiguousarray(np.asarray(inputs["L0_attn_wv"]).T),
        "woT": np.ascontiguousarray(np.asarray(inputs["L0_attn_wo"]).T),
        "sgT": np.ascontiguousarray(np.asarray(inputs["L0_attn_sg"]).T),
        "suT": np.ascontiguousarray(np.asarray(inputs["L0_attn_su"]).T),
        "sdT": np.ascontiguousarray(np.asarray(inputs["L0_attn_sd"]).T),
        "egT": np.ascontiguousarray(np.asarray(inputs["L0_attn_eg"]).transpose(0, 2, 1)),
        "euT": np.ascontiguousarray(np.asarray(inputs["L0_attn_eu"]).transpose(0, 2, 1)),
        "edT": np.ascontiguousarray(np.asarray(inputs["L0_attn_ed"]).transpose(0, 2, 1)),
        "rwT": np.ascontiguousarray(np.asarray(inputs["L0_attn_router_w"]).T),
        "ffnK": np.ascontiguousarray(np.asarray(inputs["L0_ffn_kernel"])),
        "n1pp_h": np.ascontiguousarray(
            np.asarray(inputs["L0_attn_norm1_w"]).reshape(KC, 128).T).astype(f32),
        "n2pp_h": np.ascontiguousarray(
            np.asarray(inputs["L0_attn_norm2_w"]).reshape(KC, 128).T).astype(f32),
        "biaspp_h": np.ascontiguousarray(
            np.asarray(inputs["L0_ffn_bias"]).astype(f32).reshape(KC, 128).T),
        "ebias_h": np.asarray(inputs["L0_attn_expert_bias"]).reshape(1, E).astype(f32),
        "scl_h": np.stack([np.asarray(inputs["L0_ffn_input_scale"]).reshape(()),
                         np.asarray(inputs["L0_ffn_kernel_scale"]).reshape(())]
                        ).reshape(1, 2).astype(f32),
        "cosT_h": c["cosT"], "sinT_h": c["sinT"], "mrot_h": c["mrot"], "tri_h": c["tri"],
        "ident_h": c["ident"], "econ_h": c["econ"], "onesrow_h": c["onesrow"],
        "negrow_h": c["negrow"],
        "miscrow_h": np.array([[1e-5, 1e-6, 0.0, 0.0]], np.float32),
    }
    in_maps = []
    for b in range(len(xTs)):
        m = dict(base)
        m["xT"] = xTs[b]
        in_maps.append(m)
    return in_maps


def kernel(**inputs):
    nc = build_nc(upto=7)
    in_maps = make_inputs(inputs)
    res = run_bass_kernel_spmd(nc, in_maps, core_ids=list(range(NCORES)))
    out = np.stack([np.ascontiguousarray(r["outT"].T) for r in res.results], axis=0)
    return out.astype(np.float32)


# version 20
# speedup vs baseline: 1.0090x; 1.0071x over previous
import numpy as np
import ml_dtypes
from contextlib import ExitStack

import concourse.bass as bass
import concourse.mybir as mybir
from concourse.bass_utils import run_bass_kernel_spmd

F32 = mybir.dt.float32
BF16 = mybir.dt.bfloat16
F8 = mybir.dt.float8e4
AF = mybir.ActivationFunctionType
ALU = mybir.AluOpType
AX = mybir.AxisListType

T, D, H, DH, E, NI = 2048, 1024, 16, 64, 16, 1024
KC = 8
NTF = 4
TF = 512
NTT = 16
NCORES = 8

_SEM_OF_ENGINE = {"tensor": "pe", "vector": "dv", "scalar": "ac"}


class _Op:
    __slots__ = ("engine", "fn", "deps", "sem", "val", "inc")

    def __init__(self, engine, fn, deps, sem, val, inc):
        self.engine, self.fn, self.deps = engine, fn, deps
        self.sem, self.val, self.inc = sem, val, inc


class Sched:

    def __init__(self):
        self.ops = []
        self.counts = {}
        self.bar = []
        self.last = {}

    def op(self, engine, fn, deps=(), dma=False, sem=None):
        if dma:
            sem = sem or ("ldg" if engine == "gpsimd" else "lds")
            inc = 16
        else:
            sem = _SEM_OF_ENGINE[engine]
            inc = 1
        c = self.counts.get(sem, 0) + inc
        self.counts[sem] = c
        dd = [d for d in deps if d is not None] + self.bar
        o = _Op(engine, fn, dd, sem, c, inc)
        self.ops.append(o)
        self.last[engine] = o
        return o

    def wait_only(self, engine, deps):
        o = _Op(engine, None, [d for d in deps if d is not None], None, None, 0)
        self.ops.append(o)
        return o

    def barrier(self):
        self.bar = list(self.last.values())

    def emit(self, engine_name, eng, sems):
        hw = {}
        for o in self.ops:
            if o.engine != engine_name:
                continue
            for d in o.deps:
                if hw.get(d.sem, 0) < d.val:
                    eng.wait_ge(sems[d.sem], d.val)
                    hw[d.sem] = d.val
            if o.fn is not None:
                last = o.fn(eng)
                last.then_inc(sems[o.sem], o.inc)


def _host_consts():
    bf = ml_dtypes.bfloat16
    inv = 1.0 / 10000.0 ** (np.arange(0, DH, 2, dtype=np.float64) / DH)
    fr = np.arange(T, dtype=np.float64)[:, None] * inv[None, :]
    emb = np.concatenate([fr, fr], -1)
    cosT = np.tile(np.cos(emb).T.astype(np.float32), (2, 1))
    sinT = np.tile(np.sin(emb).T.astype(np.float32), (2, 1))
    mrot = np.zeros((128, 128), np.float32)
    for m in range(128):
        if m % 64 < 32:
            mrot[(m + 32), m] = -1.0
        else:
            mrot[(m - 32), m] = 1.0
    tri = np.triu(np.ones((128, 128), np.float32))
    ident = np.eye(128, dtype=np.float32)
    econ = np.zeros((16, 16 * 128), np.float32)
    for e in range(16):
        econ[e, e * 128:(e + 1) * 128] = 1.0
    return {
        "cosT": cosT, "sinT": sinT,
        "mrot": mrot.astype(bf), "tri": tri.astype(bf), "ident": ident.astype(bf),
        "econ": econ.astype(bf),
        "onesrow": np.ones((1, 128), np.float32),
        "negrow": -np.ones((1, 128), np.float32),
        "mask2": np.stack([np.arange(128) < 64, np.arange(128) >= 64],
                          -1).astype(bf),
        "e2sel": np.concatenate([
            np.concatenate([np.ones((1, 128)), np.zeros((1, 128))], 1),
            np.concatenate([np.zeros((1, 128)), np.ones((1, 128))], 1)],
            0).astype(np.float32),
    }


def build_nc(upto=7, dbg=()):
    nc = bass.Bass("TRN2", target_bir_lowering=False)

    def din(name, shape, dt=F32):
        return nc.dram_tensor(name, list(shape), dt, kind="ExternalInput").ap()

    xT = din("xT", (D, T))
    wqT = din("wqT", (D, D)); wkT = din("wkT", (D, D)); wvT = din("wvT", (D, D))
    woT = din("woT", (D, D))
    sgT = din("sgT", (D, NI)); suT = din("suT", (D, NI)); sdT = din("sdT", (NI, D))
    egT = din("egT", (E, D, NI)); euT = din("euT", (E, D, NI)); edT = din("edT", (E, NI, D))
    rwT = din("rwT", (D, E))
    ffnK = din("ffnK", (D, D), BF16)
    n1pp_h = din("n1pp_h", (128, KC)); n2pp_h = din("n2pp_h", (128, KC))
    biaspp_h = din("biaspp_h", (128, KC))
    ebias_h = din("ebias_h", (1, E))
    scl_h = din("scl_h", (1, 2))
    cosT_h = din("cosT_h", (128, T)); sinT_h = din("sinT_h", (128, T))
    mrot_h = din("mrot_h", (128, 128), BF16); tri_h = din("tri_h", (128, 128), BF16)
    ident_h = din("ident_h", (128, 128), BF16); econ_h = din("econ_h", (16, 16 * 128), BF16)
    onesrow_h = din("onesrow_h", (1, 128)); negrow_h = din("negrow_h", (1, 128))
    miscrow_h = din("miscrow_h", (1, 4))
    mask2_h = din("mask2_h", (128, 2), BF16)
    e2sel_h = din("e2sel_h", (2, 256))

    outT = nc.dram_tensor("outT", [D, T], F32, kind="ExternalOutput").ap()
    dbg_out = {}
    for name, shape, dt in (
        ("d_xn", (D, T), BF16), ("d_h", (D, T), BF16), ("d_q", (256, T), BF16),
        ("d_k", (256, T), BF16), ("d_v", (T, 260), BF16), ("d_ctx", (D, T), BF16),
        ("d_racc", (D, T), F32), ("d_h2", (D, T), BF16), ("d_cw", (E, T), BF16),
        ("d_g", (D, T), BF16), ("d_s", (D, T), F32),
    ):
        if name in dbg:
            dbg_out[name] = nc.dram_tensor(name, list(shape), dt, kind="ExternalOutput").ap()

    s = Sched()
    store_ops = []

    def dbg_store(name, sb_ap, deps, rearr=None):
        if name in dbg_out:
            tgt = dbg_out[name]
            if rearr is not None:
                tgt = tgt.rearrange(rearr, p=128)
            o = s.op("sync", lambda e, a=sb_ap, t=tgt: e.dma_start(t, a),
                     deps=list(deps), dma=True)
            store_ops.append(o)

    esL = ExitStack()
    esR0 = ExitStack()

    def sbL(name, shape, dt):
        return esL.enter_context(nc.sbuf_tensor(name, list(shape), dt, side="left"))

    def sbR(es, name, shape, dt):
        return es.enter_context(nc.sbuf_tensor(name, list(shape), dt, side="right"))

    mrot = sbL("mrot", (128, 128), BF16)
    tri = sbL("tri", (128, 128), BF16)
    ident = sbL("ident", (128, 128), BF16)
    onescol = sbL("onescol", (128, 1), BF16)
    onesrow = sbL("onesrow", (1, 128), F32)
    negrow = sbL("negrow", (1, 128), F32)
    n1pp = sbL("n1pp", (128, KC), F32)
    n2pp = sbL("n2pp", (128, KC), F32)
    biaspp = sbL("biaspp", (128, KC), F32)
    ebias = sbL("ebias", (1, E), F32)
    scl = sbL("scl", (1, 2), F32)
    srow = sbL("srow", (1, 8), F32)
    miscrow = sbL("miscrow", (1, 4), F32)
    mask2 = sbL("mask2", (128, 2), BF16)
    e2sel = sbL("e2sel", (2, 256), F32)
    kmx2 = sbL("kmx2", (2, 8), F32)
    onecolf = sbL("onecolf", (128, 1), F32)
    spp = sbL("spp", (128, 4), F32)

    esR1 = ExitStack()
    esR2 = ExitStack()
    esRc = ExitStack()
    xnq = sbR(esR0, "xnq", (128, KC, T), F8)
    xn = sbR(esR1, "xn", (128, KC, T), BF16)
    hsb = sbR(esR2, "hsb", (128, KC, T), BF16)
    ctxT = sbR(esR2, "ctxT", (128, KC, T), BF16)
    cos_sb = sbR(esRc, "cos_sb", (128, T), F32)
    sin_sb = sbR(esRc, "sin_sb", (128, T), F32)

    ld_consts = []
    for ap_h, sb in ((cosT_h, cos_sb), (sinT_h, sin_sb), (mrot_h, mrot), (tri_h, tri),
                     (ident_h, ident), (onesrow_h, onesrow),
                     (negrow_h, negrow), (n1pp_h, n1pp), (n2pp_h, n2pp),
                     (biaspp_h, biaspp), (ebias_h, ebias), (scl_h, scl),
                     (miscrow_h, miscrow), (mask2_h, mask2), (e2sel_h, e2sel)):
        ld_consts.append(s.op("sync", lambda e, a=ap_h, t=sb: e.dma_start(t[:, :], a),
                              dma=True, sem="ldc"))
    op_ones = s.op("vector", lambda e: e.memset(onescol[:, :], 1.0))
    op_onesf = s.op("vector", lambda e: e.memset(onecolf[:, :], 1.0))
    s.bar = [ld_consts[-1], op_onesf]

    esP1 = ExitStack()
    xbf = sbR(esP1, "xbf", (128, KC, T), BF16)
    rows = sbR(esP1, "rows", (1, 4, T), F32)
    bct = sbR(esP1, "bct", (128, 3, TF), F32)
    sq_sc = sbR(esP1, "sq_sc", (128, 2, TF), BF16)
    t1_sc = sbR(esP1, "t1_sc", (128, 2, TF), F32)

    op_ldx = s.op("gpsimd", lambda e: e.dma_start(
        xbf[:, :, :], xT.rearrange("(c p) t -> p c t", p=128)), dma=True, sem="ldx")
    op_ldx2 = [op_ldx, op_ldx]

    esPs1 = ExitStack()
    ps_sum = esPs1.enter_context(nc.psum_tensor("ps_sum", [1, T], F32))
    ps_sq = esPs1.enter_context(nc.psum_tensor("ps_sq", [1, T], F32))

    sq_ops = []
    last_sq_w = {}
    for tf in range(NTF):
        sl = slice(tf * TF, (tf + 1) * TF)
        for kc in range(KC):
            par = kc % 2
            op_sq = s.op("vector", lambda e, kc=kc, sl=sl, par=par: e.tensor_tensor(
                sq_sc[:, par, :], xbf[:, kc, sl], xbf[:, kc, sl], ALU.mult),
                deps=[op_ldx2[kc // 4], op_ones, last_sq_w.get(par)])
            op_m1 = s.op("tensor", lambda e, kc=kc, sl=sl: e.matmul(
                ps_sum[0:1, sl], onescol[:, :], xbf[:, kc, sl],
                start=(kc == 0), stop=(kc == KC - 1)), deps=[op_ldx2[kc // 4], op_ones])
            op_m2 = s.op("tensor", lambda e, par=par, sl=sl, kc=kc: e.matmul(
                ps_sq[0:1, sl], onescol[:, :], sq_sc[:, par, :],
                start=(kc == 0), stop=(kc == KC - 1)), deps=[op_sq])
            last_sq_w[par] = op_m2
            if kc == KC - 1:
                sq_ops.append((op_m1, op_m2))
    stat_deps = [o for pair in sq_ops for o in pair]
    r = lambda i: rows[0:1, i, :]
    op_mean = s.op("scalar", lambda e: e.activation(r(0), ps_sum[0:1, :], AF.Copy,
                   bias=0.0, scale=1.0 / D), deps=stat_deps)
    op_msq = s.op("scalar", lambda e: e.activation(r(2), ps_sq[0:1, :], AF.Copy,
                  bias=0.0, scale=1.0 / D), deps=stat_deps)
    op_m2r = s.op("vector", lambda e: e.tensor_tensor(r(1), r(0), r(0), ALU.mult),
                  deps=[op_mean])
    op_var = s.op("vector", lambda e: e.tensor_tensor(r(1), r(2), r(1), ALU.subtract),
                  deps=[op_msq, op_m2r])
    op_lnv = s.op("scalar", lambda e: e.activation(r(2), r(1), AF.Ln,
                  bias=miscrow[0:1, 0:1], scale=1.0), deps=[op_var] + ld_consts[-1:])
    op_a = s.op("scalar", lambda e: e.activation(r(3), r(2), AF.Exp, bias=0.0,
                scale=-0.5), deps=[op_lnv])
    op_b = s.op("vector", lambda e: e.tensor_tensor(r(0), r(0), r(3), ALU.mult),
                deps=[op_a])
    op_e1 = s.op("scalar", lambda e: e.activation(r(2), r(2), AF.Exp, bias=0.0,
                 scale=-1.0), deps=[op_lnv, op_a])
    op_ratio = s.op("vector", lambda e: e.tensor_tensor(r(1), r(1), r(2), ALU.mult),
                    deps=[op_e1, op_var])
    op_lnr = s.op("scalar", lambda e: e.activation(r(1), r(1), AF.Ln,
                  bias=miscrow[0:1, 1:2], scale=1.0), deps=[op_ratio])
    op_rms = s.op("scalar", lambda e: e.activation(r(1), r(1), AF.Exp, bias=0.0,
                  scale=-0.5), deps=[op_lnr])
    esPs1.close()

    es_bc = ExitStack()
    ps_bc = [es_bc.enter_context(nc.psum_tensor(f"ps_bc{i}", [128, TF], F32))
             for i in range(2)]
    op_xn_done = []
    last_t1 = {}
    last_bct = None
    for tf in range(NTF):
        sl = slice(tf * TF, (tf + 1) * TF)
        bc_ops = []
        for i, (row_i, dep) in enumerate(((3, op_a), (0, op_b), (1, op_rms))):
            par = i % 2
            op_bm = s.op("tensor", lambda e, row_i=row_i, sl=sl, par=par: e.matmul(
                ps_bc[par][:, :], onesrow[0:1, :], rows[0:1, row_i, sl],
                start=True, stop=True), deps=[dep, bc_ops[-2] if i >= 2 else None,
                                              last_bct])
            op_bc = s.op("vector", lambda e, i=i, sl=sl, par=par: e.tensor_copy(
                bct[:, i, :], ps_bc[par][:, :]), deps=[op_bm])
            bc_ops.append(op_bc)
        for kc in range(KC):
            par = kc % 2
            op_t1 = s.op("vector", lambda e, kc=kc, sl=sl, par=par: e.tensor_tensor(
                t1_sc[:, par, :], xbf[:, kc, sl], bct[:, 0, :], ALU.mult),
                deps=bc_ops + [last_t1.get(par)])
            op_xn = s.op("vector", lambda e, kc=kc, sl=sl, par=par: e.tensor_tensor(
                xn[:, kc, sl], t1_sc[:, par, :], bct[:, 1, :], ALU.subtract),
                deps=[op_t1])
            last_t1[par] = op_xn
            op_h = s.op("vector", lambda e, kc=kc, sl=sl: e.scalar_tensor_tensor(
                hsb[:, kc, sl], bct[:, 2, :], n1pp[:, kc:kc + 1], xn[:, kc, sl],
                ALU.mult, ALU.mult), deps=[op_xn])
            op_xn_done.append(op_h)
            last_bct = op_h
    dbg_store("d_xn", xn[:, :, :], op_xn_done, "(c p) t -> p c t")
    dbg_store("d_h", hsb[:, :, :], op_xn_done, "(c p) t -> p c t")
    es_bc.close()
    s.barrier()
    esP1.close()
    if upto <= 1:
        return _finish(nc, s, store_ops, outT, xn, dbg_out, esL, esR0,
                       (esRc, esR2, esR1))

    qkv_done = []
    esQ = ExitStack()
    wq_sb = sbR(esQ, "wq_sb", (128, KC, 256), BF16)
    wk_sb = sbR(esQ, "wk_sb", (128, KC, 256), BF16)
    wv_sb = sbR(esQ, "wv_sb", (128, KC, 256), BF16)
    qrot = sbR(esQ, "qrot", (128, 2, T), BF16)
    krot = sbR(esQ, "krot", (128, 2, T), BF16)
    vsb = sbR(esQ, "vsb", (128, NTT, 260), BF16)
    qtmp = sbR(esQ, "qtmp", (128, T), BF16)
    rtmp = sbR(esQ, "rtmp", (128, 4, TF), F32)
    pt_sc = sbR(esQ, "pt_sc", (128, 8, TF), BF16)
    chpp = sbR(esQ, "chpp", (128, 4), F32)
    qsq = sbR(esQ, "qsq", (128, T), BF16)
    lb_sb = sbR(esQ, "lb_sb", (64, TF), F32)
    lrow = sbR(esQ, "lrow", (1, TF), F32)
    kmx = sbR(esQ, "kmx", (1, 16), F32)
    for quarter in range(4):
        ecols = slice(quarter * 256, (quarter + 1) * 256)
        ld_q = s.op("gpsimd", lambda e, ecols=ecols, t=wq_sb: e.dma_start(
            t[:, :, :], wqT.rearrange("(c p) e -> p c e", p=128)[:, :, ecols]),
            dma=True, sem="lwq")
        ld_k = s.op("gpsimd", lambda e, ecols=ecols, t=wk_sb: e.dma_start(
            t[:, :, :], wkT.rearrange("(c p) e -> p c e", p=128)[:, :, ecols]),
            dma=True, sem="lwk")
        ld_v = s.op("gpsimd", lambda e, ecols=ecols, t=wv_sb: e.dma_start(
            t[:, :, :], wvT.rearrange("(c p) e -> p c e", p=128)[:, :, ecols]),
            dma=True, sem="lwv")

        esPq1 = ExitStack()
        ps_mm = [esPq1.enter_context(nc.psum_tensor(f"ps_mm{quarter}_{i}", [128, TF], F32))
                 for i in range(2)]
        ps_rot = [esPq1.enter_context(nc.psum_tensor(f"ps_rot{quarter}_{i}", [128, TF], F32))
                  for i in range(2)]
        last_mm = {}
        last_rot = {}
        last_qtmp = {}
        rope_done = []
        pend_r = None
        for which, wsb, ldw, rot_dst in (("q", wq_sb, ld_q, qrot), ("k", wk_sb, ld_k, krot)):
            for ml in range(2):
                for tf in range(NTF):
                    sl = slice(tf * TF, (tf + 1) * TF)
                    par = tf % 2
                    def mmfn(e, wsb=wsb, ml=ml, sl=sl, par=par):
                        for kc in range(KC):
                            i = e.matmul(ps_mm[par][:, :],
                                         wsb[:, kc, ml * 128:(ml + 1) * 128],
                                         hsb[:, kc, sl],
                                         start=(kc == 0), stop=(kc == KC - 1))
                        return i
                    op_mm = s.op("tensor", mmfn,
                                 deps=[ldw, last_mm.get(par)] + op_xn_done[-1:])
                    op_ev = s.op("vector", lambda e, sl=sl, par=par: e.tensor_copy(
                        qtmp[:, sl], ps_mm[par][:, :]),
                        deps=[op_mm, last_qtmp.get(tf)])
                    last_mm[par] = op_ev
                    if pend_r is not None:
                        pend_r()
                        pend_r = None

                    def mk_rot(sl=sl, par=par, tf=tf, ml=ml, rot_dst=rot_dst,
                               op_ev=op_ev):
                        op_pr = s.op("tensor", lambda e: e.matmul(
                            ps_rot[par][:, :], mrot[:, :], qtmp[:, sl],
                            start=True, stop=True), deps=[op_ev, last_rot.get(par)])
                        last_qtmp[tf] = op_pr
                        op_sin = s.op("vector", lambda e: e.tensor_tensor(
                            rtmp[:, par, :], ps_rot[par][:, :], sin_sb[:, sl], ALU.mult),
                            deps=[op_pr])
                        last_rot[par] = op_sin
                        op_cos = s.op("vector", lambda e: e.tensor_tensor(
                            rtmp[:, 2 + par, :], qtmp[:, sl], cos_sb[:, sl], ALU.mult),
                            deps=[op_ev])
                        op_fin = s.op("vector", lambda e: e.tensor_tensor(
                            rot_dst[:, ml, sl], rtmp[:, 2 + par, :],
                            rtmp[:, par, :], ALU.add), deps=[op_sin, op_cos])
                        rope_done.append(op_fin)
                    pend_r = mk_rot
        pend_r()
        pend_r = None
        esPq1.close()
        esPq = ExitStack()
        ps_v = [esPq.enter_context(nc.psum_tensor(f"ps_v{quarter}_{i}", [128, 256], F32))
                for i in range(2)]
        ps_n = esPq.enter_context(nc.psum_tensor(f"ps_nn{quarter}", [2, 2, TF], F32))
        v_done = []
        last_v = {}
        op_vones = s.op("vector", lambda e: e.memset(
            vsb.ap().rearrange("p c (h x) -> p c h x", x=65)[:, :, :, 64:65], 1.0))
        ch_done = [None] * 4
        last_red = None
        last_nq = {}
        prev_lnb = None
        prev_chb = None

        def norm_pair(pairi):
            nonlocal last_red, prev_lnb, prev_chb
            ml = pairi
            for src_rot, outoff in ((krot, 0), (qrot, 1)):
                op_sq = s.op("vector", lambda e, ml=ml, src_rot=src_rot:
                             e.tensor_tensor(qsq[:, :], src_rot[:, ml, :],
                                             src_rot[:, ml, :], ALU.mult),
                             deps=rope_done + [last_red])
                reds = []
                for tf in range(NTF):
                    nb = tf % 2
                    op_nm = s.op("tensor", lambda e, tf=tf, nb=nb:
                                 e.matmul(ps_n[0:2, nb, :], mask2[:, :],
                                          qsq[:, tf * TF:(tf + 1) * TF],
                                          start=True, stop=True),
                                 deps=[op_sq, last_nq.get(nb)])
                    op_rd = s.op("vector", lambda e, tf=tf, nb=nb:
                                 e.tensor_reduce(
                        kmx2[0:2, 2 + tf:3 + tf], ps_n[0:2, nb, :], AX.X, ALU.max),
                        deps=[op_nm, prev_lnb])
                    last_nq[nb] = op_rd
                    reds.append(op_rd)
                op_mx = s.op("vector", lambda e, outoff=outoff: e.tensor_reduce(
                    kmx2[0:2, outoff:outoff + 1], kmx2[0:2, 2:6], AX.X, ALU.max),
                    deps=reds)
                last_red = op_mx
            op_lnb = s.op("scalar", lambda e: e.activation(
                kmx2[0:2, 6:8], kmx2[0:2, 0:2], AF.Ln, bias=0.0, scale=1.0),
                deps=[last_red, prev_chb])
            prev_lnb = op_lnb
            op_ch1 = s.op("vector", lambda e: e.tensor_tensor(
                kmx2[0:2, 2:3], kmx2[0:2, 6:7], kmx2[0:2, 7:8], ALU.add),
                deps=[op_lnb, prev_chb])
            op_ch = s.op("vector", lambda e: e.tensor_scalar(
                kmx2[0:2, 2:3], kmx2[0:2, 2:3], -0.5, float(np.log(8.0)),
                ALU.mult, ALU.add), deps=[op_ch1])
            for hb in range(2):
                hl = 2 * pairi + hb
                op_chb = s.op("tensor", lambda e, hb=hb: e.matmul(
                    ps_v[0][:, 0:1], e2sel[0:2, hb * 128:(hb + 1) * 128],
                    kmx2[0:2, 2:3], start=True, stop=True),
                    deps=[op_ch, last_v.get(0)])
                op_chc = s.op("vector", lambda e, hl=hl: e.tensor_copy(
                    chpp[:, hl:hl + 1], ps_v[0][:, 0:1]), deps=[op_chb])
                last_v[0] = op_chc
                ch_done[hl] = (op_chc, op_lnb, op_chb)
                prev_chb = op_chb

        for tt in range(NTT):
            par = tt % 2
            def vfn(e, tt=tt, par=par):
                for kc in range(KC):
                    i = e.matmul(ps_v[par][:, :], hsb[:, kc, tt * 128:(tt + 1) * 128],
                                 wv_sb[:, kc, :], start=(kc == 0), stop=(kc == KC - 1))
                return i
            op_vm = s.op("tensor", vfn,
                         deps=[ld_v, last_v.get(par), rope_done[-1]])
            def vefn(e, tt=tt, par=par):
                for hb in range(4):
                    i = e.tensor_copy(vsb[:, tt, hb * 65:hb * 65 + 64],
                                      ps_v[par][:, hb * 64:(hb + 1) * 64])
                return i
            op_ve = s.op("vector", vefn, deps=[op_vm, op_vones])
            last_v[par] = op_ve
            v_done.append(op_ve)
            if tt == 7:
                norm_pair(0)
            elif tt == 15:
                norm_pair(1)
        if quarter == 0:
            dbg_store("d_v", vsb[:, :, :], v_done, "(c p) e -> p c e")
            dbg_store("d_q", qrot[:, :, :], rope_done, "(c p) t -> p c t")
            dbg_store("d_k", krot[:, :, :], rope_done, "(c p) t -> p c t")
        esPq.close()

        norms_all = ch_done[3][0]
        esPa = ExitStack()
        ps_st = [esPa.enter_context(nc.psum_tensor(f"ps_st{quarter}_{i}", [128, TF], F32))
                 for i in range(4)]
        ps_cl = [esPa.enter_context(nc.psum_tensor(f"ps_cl{quarter}_{i}", [65, TF], F32))
                 for i in range(2)]
        ps_lb = esPa.enter_context(nc.psum_tensor(f"ps_lb{quarter}", [64, TF], F32))

        last_stpar = {i: norms_all for i in range(4)}
        last_pt = {}
        evac_done = [norms_all, norms_all]
        last_lb = [norms_all, norms_all]
        evac_pend = None
        for pairi in range(2):
            ml = pairi
            heads = (2 * pairi, 2 * pairi + 1)
            chcs = {hb: ch_done[2 * pairi + hb][0] for hb in range(2)}
            for qc in range(NTF):
                qbase = qc * TF
                nkt = 4 * qc + 4
                clp = qc % 2
                pend = []
                for kt in range(nkt):
                    j = kt - 4 * qc
                    off = max(0, j) * 128
                    ssl = slice(off, TF)
                    qsl = slice(qbase + off, qbase + TF)
                    prs = []
                    for hb in range(2):
                        hl = 2 * pairi + hb
                        rsl = slice(hb * 64, hb * 64 + 64)
                        spar = 2 * hb + (kt % 2)
                        ptx = 4 * hb + (kt % 4)
                        op_st = s.op("tensor", lambda e, rsl=rsl, ml=ml, kt=kt,
                                     spar=spar, ssl=ssl, qsl=qsl: e.matmul(
                            ps_st[spar][:, ssl], krot[rsl, ml, kt * 128:(kt + 1) * 128],
                            qrot[rsl, ml, qsl], start=True, stop=True),
                            deps=[last_stpar.get(spar), chcs[hb]])
                        if kt == 2 and hb == 0 and evac_pend is not None:
                            evac_pend()
                            evac_pend = None
                        op_exp = s.op("scalar", lambda e, spar=spar, ptx=ptx,
                                      ssl=ssl, hl=hl: e.activation(
                            pt_sc[:, ptx, ssl], ps_st[spar][:, ssl], AF.Exp,
                            bias=chpp[:, hl:hl + 1], scale=0.125),
                            deps=[op_st, last_pt.get(ptx)])
                        last_stpar[spar] = op_exp
                        pt_ready = op_exp
                        if j >= 0:
                            pt_ready = s.op("vector", lambda e, ptx=ptx, off=off:
                                            e.tensor_tensor(
                                pt_sc[:, ptx, off:off + 128],
                                pt_sc[:, ptx, off:off + 128],
                                tri[:, :], ALU.mult), deps=[op_exp])
                        prs.append(pt_ready)

                    def mk_lav(prs=prs, ssl=ssl, kt=kt, nkt=nkt, pairi=pairi,
                               clp=clp):
                        o_avs = []
                        for hb in range(2):
                            hl = 2 * pairi + hb
                            ptx = 4 * hb + (kt % 4)
                            o_av = s.op("tensor", lambda e, ptx=ptx, ssl=ssl,
                                        kt=kt, nkt=nkt, hl=hl, hb=hb, clp=clp:
                                        e.matmul(
                                ps_cl[hb][:, ssl], vsb[:, kt, hl * 65:(hl + 1) * 65],
                                pt_sc[:, ptx, ssl], start=(kt == 0),
                                stop=(kt == nkt - 1)),
                                deps=[prs[hb]] + ([evac_done[hb]] if kt == 0 else [])
                                     + v_done)
                            last_pt[ptx] = o_av
                            o_avs.append(o_av)
                        mk_lav.res = o_avs
                    if len(pend) >= 2:
                        pend.pop(0)()
                    pend.append(mk_lav)
                if evac_pend is not None:
                    evac_pend()
                    evac_pend = None
                for fn in pend:
                    fn()
                av_lasts = fn.res

                def mk_evac(av_lasts=av_lasts, pairi=pairi, qbase=qbase, hh0=quarter * 4 + 2 * pairi):
                    for hb in range(2):
                        hh = hh0 + hb
                        r0 = hb * 64
                        op_ln_l = s.op("scalar", lambda e, hb=hb: e.activation(
                            lrow[0:1, :], ps_cl[hb][64:65, :], AF.Ln, bias=0.0,
                            scale=1.0), deps=[av_lasts[hb], last_lb[0]])
                        op_linv = s.op("scalar", lambda e: e.activation(
                            lrow[0:1, :], lrow[0:1, :], AF.Exp, bias=0.0, scale=-1.0),
                            deps=[op_ln_l])
                        op_lb = s.op("tensor", lambda e: e.matmul(
                            ps_lb[:, :], onesrow[0:1, 0:64], lrow[0:1, :],
                            start=True, stop=True), deps=[op_linv, last_lb[1]])
                        op_lbc = s.op("vector", lambda e: e.tensor_copy(
                            lb_sb[:, :], ps_lb[:, :]),
                            deps=[op_lb, evac_done[1 - hb], evac_done[hb]])
                        op_cev = s.op("vector", lambda e, r0=r0, hh=hh, qbase=qbase,
                                      hb=hb: e.tensor_tensor(
                            ctxT[r0:r0 + 64, hh // 2, qbase:qbase + TF],
                            ps_cl[hb][0:64, :], lb_sb[:, :], ALU.mult),
                            deps=[op_lbc, av_lasts[hb]])
                        evac_done[hb] = op_cev
                        last_lb[0] = op_lb
                        last_lb[1] = op_lbc
                        qkv_done.append(op_cev)
                evac_pend = mk_evac
        if evac_pend is not None:
            evac_pend()
            evac_pend = None
        esPa.close()
        s.barrier()
    esQ.close()
    dbg_store("d_ctx", ctxT[:, :, :], [s.last["vector"]], "(c p) t -> p c t")
    if upto <= 2:
        return _finish(nc, s, store_ops, outT, xn, dbg_out, esL, esR0,
                       (esRc, esR2, esR1))

    esRc.close()
    racc = sbL("racc", (128, KC, T), F32)
    esP3 = ExitStack()
    wo_sb = sbR(esP3, "wo_sb", (128, KC, D), BF16)
    ld_wo = s.op("gpsimd", lambda e: e.dma_start(
        wo_sb[:, :, :], woT.rearrange("(c p) e -> p c e", p=128)), dma=True, sem="lwo")
    esPo = ExitStack()
    ps_o = [esPo.enter_context(nc.psum_tensor(f"ps_o{i}", [128, TF], F32))
            for i in range(2)]
    last_o = {}
    racc_done = []
    for tf in range(NTF):
        sl = slice(tf * TF, (tf + 1) * TF)
        for mc in range(KC):
            par = (mc * NTF + tf) % 2
            def ofn(e, mc=mc, sl=sl, par=par):
                for kc in range(KC):
                    i = e.matmul(ps_o[par][:, :], wo_sb[:, kc, mc * 128:(mc + 1) * 128],
                                 ctxT[:, kc, sl], start=(kc == 0), stop=(kc == KC - 1))
                return i
            op_om = s.op("tensor", ofn, deps=[ld_wo, last_o.get(par)])
            op_oe = s.op("vector", lambda e, mc=mc, sl=sl, par=par: e.tensor_tensor(
                racc[:, mc, sl], ps_o[par][:, :], xn[:, mc, sl], ALU.add),
                deps=[op_om])
            last_o[par] = op_oe
            racc_done.append(op_oe)

    dbg_store("d_racc", racc[:, :, :], racc_done, "(c p) t -> p c t")
    esPo.close()
    s.barrier()
    esP3.close()
    esR2.close()
    if upto <= 3:
        return _finish(nc, s, store_ops, outT, racc, dbg_out, esL, esR0,
                       (esR2, esR1))

    op_abs = s.op("scalar", lambda e: e.activation(srow[0:1, 2:4], scl[0:1, 0:2],
                  AF.Abs, bias=0.0, scale=1.0), deps=ld_consts)
    op_lns = s.op("scalar", lambda e: e.activation(srow[0:1, 2:4], srow[0:1, 2:4],
                  AF.Ln, bias=0.0, scale=1.0), deps=[op_abs])
    op_exs = s.op("scalar", lambda e: e.activation(srow[0:1, 2:4], srow[0:1, 2:4],
                  AF.Exp, bias=0.0, scale=-1.0), deps=[op_lns])
    op_sgn = s.op("scalar", lambda e: e.activation(srow[0:1, 6:8], scl[0:1, 0:2],
                  AF.Sign, bias=0.0, scale=1.0), deps=[op_exs])
    op_rr = s.op("vector", lambda e: e.tensor_tensor(
        srow[0:1, 2:4], srow[0:1, 2:4], srow[0:1, 6:8], ALU.mult), deps=[op_sgn])
    op_sp = s.op("vector", lambda e: e.tensor_tensor(
        srow[0:1, 4:5], scl[0:1, 0:1], scl[0:1, 1:2], ALU.mult), deps=[op_rr])
    esPsc = ExitStack()
    ps_sc = esPsc.enter_context(nc.psum_tensor("ps_sc", [128, 4], F32))
    op_spm = s.op("tensor", lambda e: e.matmul(
        ps_sc[:, 0:3], onesrow[0:1, :], srow[0:1, 2:5],
        start=True, stop=True), deps=[op_sp])
    op_spc = s.op("vector", lambda e: e.tensor_copy(spp[:, 0:3], ps_sc[:, 0:3]),
                  deps=[op_spm])
    esPsc.close()
    xnq_done = []
    for mc in range(KC):
        op_q8 = s.op("scalar", lambda e, mc=mc: e.activation(
            xnq[:, mc, :], xn[:, mc, :], AF.Copy, bias=0.0, scale=spp[:, 0:1]),
            deps=[op_spc])
        xnq_done.append(op_q8)
    esR1.close()

    esP5 = ExitStack()
    NSLOT = 4
    slots = [sbR(esP5, f"wslot{i}", (128, KC, 1024), BF16) for i in range(NSLOT)]
    mats = [(sgT, None), (suT, None), (sdT, None)]
    for e_i in range(E):
        mats += [(egT, e_i), (euT, e_i), (edT, e_i)]
    dma_ev = [None] * len(mats)
    cons_ev = [None] * len(mats)

    def append_dma(idx):
        src_, e_i = mats[idx]
        slot = slots[idx % NSLOT]
        if e_i is None:
            apsrc = src_.rearrange("(c p) n -> p c n", p=128)
        else:
            apsrc = src_[e_i].rearrange("(c p) n -> p c n", p=128)
        dma_ev[idx] = s.op("gpsimd", lambda e, t=slot, a=apsrc: e.dma_start(
            t[:, :, :], a),
            deps=[cons_ev[idx - NSLOT]] if idx >= NSLOT else [xnq_done[-1]],
            dma=True, sem=f"ls{idx % 4}")

    for _i in range(NSLOT):
        dma_ev[_i] = None
    def _start_dmas():
        for _i in range(NSLOT):
            append_dma(_i)
    _sd_done = False
    _start_dmas()
    h2 = sbL("h2", (128, KC, T), BF16)
    econ = sbL("econ", (16, 16 * 128), BF16)
    cwT = sbL("cwT", (16, T), BF16)
    ebias_bc = sbL("ebias_bc", (128, E), F32)
    ld_econ = s.op("sync", lambda e: e.dma_start(econ[:, :], econ_h), dma=True,
                   sem="lde")
    esP4 = ExitStack()
    rows2 = sbR(esP4, "rows2", (1, 1, T), F32)
    rsq_sc4 = sbR(esP4, "rsq_sc4", (128, 2, TF), BF16)
    r2bc = sbR(esP4, "r2bc", (128, T), F32)
    gsc = sbR(esP4, "gsc", (128, 4, E), F32)
    cwbf = sbR(esP4, "cwbf", (128, E), BF16)
    tcol = sbR(esP4, "tcol", (128, 4), F32)
    esPs4 = ExitStack()
    ps_sq2 = esPs4.enter_context(nc.psum_tensor("ps_sq2", [1, T], F32))
    last_rsq = {}
    sq2_ops = []
    stat_pend = []
    for tf in range(NTF):
        sl = slice(tf * TF, (tf + 1) * TF)
        for mc in range(KC):
            def mk_stat(mc=mc, sl=sl):
                spar = mc % 2
                op_r2 = s.op("vector", lambda e, mc=mc, sl=sl, spar=spar:
                             e.tensor_tensor(
                    rsq_sc4[:, spar, :], racc[:, mc, sl], racc[:, mc, sl], ALU.mult),
                    deps=[last_rsq.get(spar)])
                op_sm = s.op("tensor", lambda e, sl=sl, spar=spar, mc=mc: e.matmul(
                    ps_sq2[0:1, sl], onescol[:, 0:1], rsq_sc4[:, spar, :],
                    start=(mc == 0), stop=(mc == KC - 1)), deps=[op_r2])
                last_rsq[spar] = op_sm
                if mc == KC - 1:
                    sq2_ops.append(op_sm)
            if len(stat_pend) >= 2:
                stat_pend.pop(0)()
            stat_pend.append(mk_stat)
    for fn in stat_pend:
        fn()
    op_ms2 = s.op("scalar", lambda e: e.activation(rows2[0:1, 0, :], ps_sq2[0:1, :],
                  AF.Copy, bias=0.0, scale=1.0 / D), deps=sq2_ops)
    op_ln2 = s.op("scalar", lambda e: e.activation(rows2[0:1, 0, :], rows2[0:1, 0, :],
                  AF.Ln, bias=miscrow[0:1, 1:2], scale=1.0), deps=[op_ms2])
    op_rm2 = s.op("scalar", lambda e: e.activation(rows2[0:1, 0, :], rows2[0:1, 0, :],
                  AF.Exp, bias=0.0, scale=-0.5), deps=[op_ln2])
    esPs4.close()
    es_bc2 = ExitStack()
    ps_bc2 = [es_bc2.enter_context(nc.psum_tensor(f"ps_bc2{i}", [128, TF], F32))
              for i in range(2)]
    bc2_done = []
    last_bc2 = {}
    for tf in range(NTF):
        sl = slice(tf * TF, (tf + 1) * TF)
        par = tf % 2
        op_bm = s.op("tensor", lambda e, sl=sl, par=par: e.matmul(
            ps_bc2[par][:, :], onesrow[0:1, :], rows2[0:1, 0, sl], start=True, stop=True),
            deps=[op_rm2, last_bc2.get(par)])
        op_bc = s.op("vector", lambda e, sl=sl, par=par: e.tensor_copy(
            r2bc[:, sl], ps_bc2[par][:, :]), deps=[op_bm])
        last_bc2[par] = op_bc
        bc2_done.append(op_bc)
    h2_done = []
    for tf in range(NTF):
        sl = slice(tf * TF, (tf + 1) * TF)
        for mc in range(KC):
            op_h2 = s.op("vector", lambda e, mc=mc, sl=sl: e.scalar_tensor_tensor(
                h2[:, mc, sl], r2bc[:, sl], n2pp[:, mc:mc + 1], racc[:, mc, sl],
                ALU.mult, ALU.mult), deps=bc2_done)
            h2_done.append(op_h2)
    dbg_store("d_h2", h2[:, :, :], h2_done, "(c p) t -> p c t")
    es_bc2.close()

    rw_sb = sbR(esP4, "rw_sb", (128, KC, E), BF16)
    ld_rw = s.op("gpsimd", lambda e: e.dma_start(
        rw_sb[:, :, :], rwT.rearrange("(c p) e -> p c e", p=128)), dma=True, sem="lwr")
    esPr = ExitStack()
    ps_lg = [esPr.enter_context(nc.psum_tensor(f"ps_lg{i}", [128, E], F32))
             for i in range(2)]
    ps_tr = esPr.enter_context(nc.psum_tensor("ps_tr", [16, 128], BF16))
    op_ebm = s.op("tensor", lambda e: e.matmul(
        ps_lg[0][:, :], onesrow[0:1, :], ebias[0:1, :], start=True, stop=True),
        deps=bc2_done)
    op_ebc = s.op("vector", lambda e: e.tensor_copy(ebias_bc[:, :], ps_lg[0][:, :]),
                  deps=[op_ebm])
    last_lg = {0: op_ebc}
    last_tr = None
    prev_gg = prev_de = None
    cw_done = []
    for tt in range(NTT):
        par = tt % 2
        def lgfn(e, tt=tt, par=par):
            for kc in range(KC):
                i = e.matmul(ps_lg[par][:, :], h2[:, kc, tt * 128:(tt + 1) * 128],
                             rw_sb[:, kc, :], start=(kc == 0), stop=(kc == KC - 1))
            return i
        op_lg = s.op("tensor", lgfn,
                     deps=[ld_rw, last_lg.get(par)]
                          + h2_done[8 * (tt // 4):8 * (tt // 4) + 8])
        op_ga = s.op("vector", lambda e, par=par: e.tensor_tensor(
            gsc[:, 0, :], ps_lg[par][:, :], ebias_bc[:, :], ALU.add),
            deps=[op_lg, op_ebc, prev_gg, prev_de])
        last_lg[par] = op_ga
        op_gx = s.op("scalar", lambda e: e.activation(gsc[:, 1, :], gsc[:, 0, :],
                     AF.Exp, bias=0.0, scale=-1.0), deps=[op_ga])
        op_gl = s.op("scalar", lambda e: e.activation(gsc[:, 1, :], gsc[:, 1, :],
                     AF.Ln, bias=onecolf[:, 0:1], scale=1.0), deps=[op_gx, op_onesf])
        op_gg = s.op("scalar", lambda e: e.activation(gsc[:, 0, :], gsc[:, 1, :],
                     AF.Exp, bias=0.0, scale=-1.0), deps=[op_gl])
        op_m1 = s.op("vector", lambda e: e.tensor_reduce(
            tcol[:, 0:1], gsc[:, 0, :], AX.X, ALU.max), deps=[op_gg])
        op_i1 = s.op("vector", lambda e: e.tensor_scalar(
            gsc[:, 1, :], gsc[:, 0, :], tcol[:, 0:1], None, ALU.is_ge), deps=[op_m1])
        op_g2 = s.op("vector", lambda e: e.scalar_tensor_tensor(
            gsc[:, 2, :], gsc[:, 1, :], -1e9, gsc[:, 0, :], ALU.mult, ALU.add),
            deps=[op_i1])
        op_m2 = s.op("vector", lambda e: e.tensor_reduce(
            tcol[:, 1:2], gsc[:, 2, :], AX.X, ALU.max), deps=[op_g2])
        op_i2 = s.op("vector", lambda e: e.tensor_scalar(
            gsc[:, 2, :], gsc[:, 2, :], tcol[:, 1:2], None, ALU.is_ge), deps=[op_m2])
        op_dn = s.op("vector", lambda e: e.tensor_tensor(
            tcol[:, 2:3], tcol[:, 0:1], tcol[:, 1:2], ALU.add), deps=[op_m2])
        op_dn2 = s.op("vector", lambda e: e.tensor_scalar(
            tcol[:, 2:3], tcol[:, 2:3], 1e-9, None, ALU.add), deps=[op_dn])
        op_dl = s.op("scalar", lambda e: e.activation(tcol[:, 2:3], tcol[:, 2:3],
                     AF.Ln, bias=0.0, scale=1.0), deps=[op_dn2])
        op_de = s.op("scalar", lambda e: e.activation(tcol[:, 2:3], tcol[:, 2:3],
                     AF.Exp, bias=0.0, scale=-1.0), deps=[op_dl])
        op_w1 = s.op("vector", lambda e: e.tensor_tensor(
            tcol[:, 0:1], tcol[:, 0:1], tcol[:, 2:3], ALU.mult), deps=[op_de])
        op_w2 = s.op("vector", lambda e: e.tensor_tensor(
            tcol[:, 1:2], tcol[:, 1:2], tcol[:, 2:3], ALU.mult), deps=[op_de])
        op_cw1 = s.op("vector", lambda e: e.tensor_scalar(
            gsc[:, 3, :], gsc[:, 1, :], tcol[:, 0:1], None, ALU.mult), deps=[op_w1, op_i2])
        op_cw = s.op("vector", lambda e: e.scalar_tensor_tensor(
            cwbf[:, :], gsc[:, 2, :], tcol[:, 1:2], gsc[:, 3, :], ALU.mult, ALU.add),
            deps=[op_cw1, op_w2])
        op_tr = s.op("tensor", lambda e: e.transpose(
            ps_tr[:, :], cwbf[:, :], ident[:, :]), deps=[op_cw, last_tr])
        op_tc = s.op("vector", lambda e, tt=tt: e.tensor_copy(
            cwT[0:16, tt * 128:(tt + 1) * 128], ps_tr[:, :]), deps=[op_tr])
        last_tr = op_tc
        prev_gg, prev_de = op_gg, op_de
        cw_done.append(op_tc)
    dbg_store("d_cw", cwT[:, :], cw_done)
    esPr.close()

    s.barrier()
    esP4.close()
    if upto <= 4:
        return _finish(nc, s, store_ops, outT, racc, dbg_out, esL, esR0,
                       (esR1,))

    z_sb = sbR(esP5, "z_sb", (128, KC, TF), BF16)
    wbc = sbR(esP5, "wbc", (128, T), BF16)
    zsil = sbR(esP5, "zsil", (128, 2, TF), BF16)
    tz = sbR(esP5, "tz", (128, 2, TF), F32)

    esPm = ExitStack()
    ps_g = [esPm.enter_context(nc.psum_tensor(f"ps_g{i}", [128, TF], F32)) for i in range(2)]
    ps_u = [esPm.enter_context(nc.psum_tensor(f"ps_u{i}", [128, TF], F32)) for i in range(2)]
    ps_y = [esPm.enter_context(nc.psum_tensor(f"ps_y{i}", [128, TF], F32)) for i in range(2)]
    ps_wb = esPm.enter_context(nc.psum_tensor("ps_wb", [128, TF], F32))

    last_gp = {}
    last_up = {}
    last_yp = {}
    last_wbc = None
    last_zsil = {}
    last_tz = {}
    moe_done = []
    for set_i in range(17):
        e_i = set_i - 1
        ig, iu, idn = 3 * set_i, 3 * set_i + 1, 3 * set_i + 2
        sg_slot, su_slot, sd_slot = (slots[ig % NSLOT], slots[iu % NSLOT],
                                     slots[idn % NSLOT])
        wbc_ops = []
        if e_i >= 0:
            for tf in range(NTF):
                sl = slice(tf * TF, (tf + 1) * TF)
                op_wm = s.op("tensor", lambda e, e_i=e_i, sl=sl: e.matmul(
                    ps_wb[:, :], econ[0:16, e_i * 128:(e_i + 1) * 128], cwT[0:16, sl],
                    start=True, stop=True), deps=cw_done + [last_wbc, ld_econ])
                op_wc = s.op("vector", lambda e, sl=sl: e.tensor_copy(
                    wbc[:, sl], ps_wb[:, :]), deps=[op_wm])
                last_wbc = op_wc
                wbc_ops.append(op_wc)
        g_cons = u_cons = d_cons = None
        for tf in range(NTF):
            sl = slice(tf * TF, (tf + 1) * TF)
            z_ready = []
            for mc in range(KC):
                par = mc % 2
                def gfn(e, mc=mc, sl=sl, par=par, w=sg_slot):
                    for kc in range(KC):
                        i = e.matmul(ps_g[par][:, :], w[:, kc, mc * 128:(mc + 1) * 128],
                                     h2[:, kc, sl], start=(kc == 0), stop=(kc == KC - 1))
                    return i
                op_gm = s.op("tensor", gfn, deps=[dma_ev[ig], last_gp.get(par)])
                g_cons = op_gm
                op_sl = s.op("scalar", lambda e, par=par: e.activation(
                    zsil[:, par, :], ps_g[par][:, :], AF.Silu, bias=0.0, scale=1.0),
                    deps=[op_gm, last_zsil.get(par)])
                last_gp[par] = op_sl
                def ufn(e, mc=mc, sl=sl, par=par, w=su_slot):
                    for kc in range(KC):
                        i = e.matmul(ps_u[par][:, :], w[:, kc, mc * 128:(mc + 1) * 128],
                                     h2[:, kc, sl], start=(kc == 0), stop=(kc == KC - 1))
                    return i
                op_um = s.op("tensor", ufn, deps=[dma_ev[iu], last_up.get(par)])
                u_cons = op_um
                if e_i >= 0:
                    op_t = s.op("vector", lambda e, par=par: e.tensor_tensor(
                        tz[:, par, :], ps_u[par][:, :], zsil[:, par, :], ALU.mult),
                        deps=[op_um, op_sl, last_tz.get(par)])
                    last_up[par] = op_t
                    op_z = s.op("vector", lambda e, mc=mc, par=par, sl=sl:
                                e.tensor_tensor(z_sb[:, mc, :], tz[:, par, :],
                                                wbc[:, sl], ALU.mult),
                                deps=[op_t] + wbc_ops)
                    last_tz[par] = op_z
                else:
                    op_z = s.op("vector", lambda e, mc=mc, par=par: e.tensor_tensor(
                        z_sb[:, mc, :], ps_u[par][:, :], zsil[:, par, :], ALU.mult),
                        deps=[op_um, op_sl])
                    last_up[par] = op_z
                last_zsil[par] = op_z
                z_ready.append(op_z)
            for dc in range(KC):
                par = dc % 2
                def yfn1(e, dc=dc, par=par, w=sd_slot):
                    for ic in range(4):
                        i = e.matmul(ps_y[par][:, :], w[:, ic, dc * 128:(dc + 1) * 128],
                                     z_sb[:, ic, :], start=(ic == 0), stop=False)
                    return i
                def yfn2(e, dc=dc, par=par, w=sd_slot):
                    for ic in range(4, KC):
                        i = e.matmul(ps_y[par][:, :], w[:, ic, dc * 128:(dc + 1) * 128],
                                     z_sb[:, ic, :], start=False, stop=(ic == KC - 1))
                    return i
                s.op("tensor", yfn1, deps=z_ready[:4] + [dma_ev[idn],
                     last_yp.get(par)])
                op_ym = s.op("tensor", yfn2, deps=z_ready[4:])
                d_cons = op_ym
                op_ra = s.op("vector", lambda e, dc=dc, sl=sl, par=par: e.tensor_tensor(
                    racc[:, dc, sl], ps_y[par][:, :], racc[:, dc, sl], ALU.add),
                    deps=[op_ym])
                last_yp[par] = op_ra
                moe_done.append(op_ra)
        cons_ev[ig], cons_ev[iu], cons_ev[idn] = g_cons, u_cons, d_cons
        for j in range(3 * set_i + NSLOT, min(3 * set_i + 3 + NSLOT, len(mats))):
            if dma_ev[j] is None:
                append_dma(j)
    dbg_store("d_s", racc[:, :, :], moe_done[-4:], "(c p) t -> p c t")
    esPm.close()
    s.barrier()
    esP5.close()
    if upto <= 5:
        return _finish(nc, s, store_ops, outT, racc, dbg_out, esL, esR0)

    esR6 = ExitStack()
    g_sb = sbR(esR6, "g_sb", (128, KC, T), BF16)
    esP6 = ExitStack()
    kq = sbR(esP6, "kq", (128, KC, 1024), F8)
    fk_sb = sbR(esP6, "fk_sb", (128, KC, 1024), BF16)
    y2_sc = sbR(esP6, "y2_sc", (128, 2, TF), BF16)
    ld_fk = s.op("sync", lambda e: e.dma_start(
        fk_sb[:, :, :], ffnK.rearrange("(c p) f -> p c f", p=128)), dma=True, sem="ldf")
    kq_done = []
    for mc in range(KC):
        op_kq = s.op("scalar", lambda e, mc=mc: e.activation(
            kq[:, mc, :], fk_sb[:, mc, :], AF.Copy, bias=0.0, scale=spp[:, 1:2]),
            deps=[ld_fk, op_spc])
        kq_done.append(op_kq)
    esPf = ExitStack()
    ps_f = [esPf.enter_context(nc.psum_tensor(f"ps_f{i}", [128, TF], F32))
            for i in range(2)]
    last_f = {}
    last_sil = {}
    g_done = []
    for fc in range(KC):
        for tf in range(NTF):
            sl = slice(tf * TF, (tf + 1) * TF)
            par = (fc * NTF + tf) % 2
            def ffn_mm(e, fc=fc, sl=sl, par=par):
                for kc in range(KC):
                    i = e.matmul(ps_f[par][:, :], kq[:, kc, fc * 128:(fc + 1) * 128],
                                 xnq[:, kc, sl], start=(kc == 0), stop=(kc == KC - 1))
                return i
            op_fm = s.op("tensor", ffn_mm, deps=kq_done + xnq_done + [last_f.get(par)])
            op_y2 = s.op("vector", lambda e, fc=fc, par=par: e.tensor_scalar(
                y2_sc[:, par, :], ps_f[par][:, :], spp[:, 2:3], biaspp[:, fc:fc + 1],
                ALU.mult, ALU.add), deps=[op_fm, last_sil.get(par)])
            last_f[par] = op_y2
            op_gs = s.op("scalar", lambda e, fc=fc, sl=sl, par=par: e.activation(
                g_sb[:, fc, sl], y2_sc[:, par, :], AF.Silu, bias=0.0, scale=1.0),
                deps=[op_y2])
            last_sil[par] = op_gs
            g_done.append(op_gs)
    dbg_store("d_g", g_sb[:, :, :], g_done, "(c p) t -> p c t")
    last_ffn_mm = last_f
    esPf.close()
    esP6.close()

    esP7 = ExitStack()
    xf32 = sbR(esP7, "xf32", (128, 2, T), F32)
    tmp7 = sbR(esP7, "tmp7", (128, T), F32)
    osb = sbR(esP7, "osb", (128, 2, T), F32)
    last_x = {}
    last_os = {}
    last_tmp = None
    ffn_pe_done = [last_ffn_mm.get(0), last_ffn_mm.get(1)]
    for mc in range(KC):
        par = mc % 2
        op_lx = s.op("sync", lambda e, mc=mc, par=par: e.dma_start(
            xf32[:, par, :], xT[mc * 128:(mc + 1) * 128, :]),
            deps=[last_x.get(par)] + ffn_pe_done, dma=True, sem=f"lx{par}")
        op_sg = s.op("vector", lambda e, mc=mc: e.tensor_tensor(
            tmp7[:, :], racc[:, mc, :], g_sb[:, mc, :], ALU.mult),
            deps=[last_tmp, g_done[-1], moe_done[-1]])
        op_out = s.op("vector", lambda e, mc=mc, par=par: e.tensor_tensor(
            osb[:, par, :], tmp7[:, :], xf32[:, par, :], ALU.add),
            deps=[op_sg, op_lx, last_os.get(par)])
        last_tmp = op_out
        op_st = s.op("sync", lambda e, mc=mc, par=par: e.dma_start(
            outT[mc * 128:(mc + 1) * 128, :], osb[:, par, :]),
            deps=[op_out], dma=True, sem=f"st{par}")
        last_x[par] = op_out
        last_os[par] = op_st
        store_ops.append(op_st)
    esP7.close()
    esR6.close()
    return _finish(nc, s, store_ops, outT, racc, dbg_out, esL, esR0)


def _finish(nc, s, store_ops, outT, racc, dbg_out, esL, esR0, extra=()):
    if not any(o.sem.startswith(("lds", "st")) for o in store_ops):
        op = s.op("gpsimd", lambda e: e.dma_start(
            outT.rearrange("(c p) t -> p c t", p=128), racc[:, :, :]),
            deps=[s.last.get("vector"), s.last.get("scalar"), s.last.get("tensor")],
            dma=True)
        store_ops.append(op)
    s.wait_only("sync", store_ops)
    s.wait_only("gpsimd", [o for o in store_ops if o.sem == "ldg"])

    with ExitStack() as es:
        sems = {}
        sem_names = set()
        for o in s.ops:
            if o.sem:
                sem_names.add(o.sem)
        for o in s.ops:
            for d in o.deps:
                sem_names.add(d.sem)
        for name in sorted(sem_names):
            sems[name] = es.enter_context(nc.semaphore(name))
        block = es.enter_context(nc.Block())

        @block.tensor
        def _(eng):
            s.emit("tensor", eng, sems)

        @block.vector
        def _(eng):
            s.emit("vector", eng, sems)

        @block.scalar
        def _(eng):
            s.emit("scalar", eng, sems)

        @block.gpsimd
        def _(eng):
            s.emit("gpsimd", eng, sems)

        @block.sync
        def _(eng):
            s.emit("sync", eng, sems)
    for es_x in extra:
        try:
            es_x.close()
        except Exception:
            pass
    esR0.close()
    esL.close()
    return nc


def make_inputs(inputs, consts=None):
    f32 = np.float32
    c = consts or _host_consts()
    x = np.asarray(inputs["L0_attn_x"])
    xTs = [np.ascontiguousarray(x[b].T).astype(f32, copy=False) for b in range(x.shape[0])]
    base = {
        "wqT": np.ascontiguousarray(np.asarray(inputs["L0_attn_wq"]).T),
        "wkT": np.ascontiguousarray(np.asarray(inputs["L0_attn_wk"]).T),
        "wvT": np.ascontiguousarray(np.asarray(inputs["L0_attn_wv"]).T),
        "woT": np.ascontiguousarray(np.asarray(inputs["L0_attn_wo"]).T),
        "sgT": np.ascontiguousarray(np.asarray(inputs["L0_attn_sg"]).T),
        "suT": np.ascontiguousarray(np.asarray(inputs["L0_attn_su"]).T),
        "sdT": np.ascontiguousarray(np.asarray(inputs["L0_attn_sd"]).T),
        "egT": np.ascontiguousarray(np.asarray(inputs["L0_attn_eg"]).transpose(0, 2, 1)),
        "euT": np.ascontiguousarray(np.asarray(inputs["L0_attn_eu"]).transpose(0, 2, 1)),
        "edT": np.ascontiguousarray(np.asarray(inputs["L0_attn_ed"]).transpose(0, 2, 1)),
        "rwT": np.ascontiguousarray(np.asarray(inputs["L0_attn_router_w"]).T),
        "ffnK": np.ascontiguousarray(np.asarray(inputs["L0_ffn_kernel"])),
        "n1pp_h": np.ascontiguousarray(
            np.asarray(inputs["L0_attn_norm1_w"]).reshape(KC, 128).T).astype(f32),
        "n2pp_h": np.ascontiguousarray(
            np.asarray(inputs["L0_attn_norm2_w"]).reshape(KC, 128).T).astype(f32),
        "biaspp_h": np.ascontiguousarray(
            np.asarray(inputs["L0_ffn_bias"]).astype(f32).reshape(KC, 128).T),
        "ebias_h": np.asarray(inputs["L0_attn_expert_bias"]).reshape(1, E).astype(f32),
        "scl_h": np.stack([np.asarray(inputs["L0_ffn_input_scale"]).reshape(()),
                         np.asarray(inputs["L0_ffn_kernel_scale"]).reshape(())]
                        ).reshape(1, 2).astype(f32),
        "cosT_h": c["cosT"], "sinT_h": c["sinT"], "mrot_h": c["mrot"], "tri_h": c["tri"],
        "ident_h": c["ident"], "econ_h": c["econ"], "onesrow_h": c["onesrow"],
        "negrow_h": c["negrow"],
        "miscrow_h": np.array([[1e-5, 1e-6, 0.0, 0.0]], np.float32),
        "mask2_h": c["mask2"], "e2sel_h": c["e2sel"],
    }
    in_maps = []
    for b in range(len(xTs)):
        m = dict(base)
        m["xT"] = xTs[b]
        in_maps.append(m)
    return in_maps


def kernel(**inputs):
    nc = build_nc(upto=7)
    in_maps = make_inputs(inputs)
    res = run_bass_kernel_spmd(nc, in_maps, core_ids=list(range(NCORES)))
    out = np.stack([np.ascontiguousarray(r["outT"].T) for r in res.results], axis=0)
    return out.astype(np.float32)
